# version 18
# baseline (speedup 1.0000x reference)
import numpy as np
from contextlib import ExitStack

import concourse.bass as bass
import concourse.mybir as mybir
from concourse.bass_utils import run_bass_kernel_spmd

F32 = mybir.dt.float32
BF16 = mybir.dt.bfloat16
I32 = mybir.dt.int32
AF = mybir.ActivationFunctionType
ALU = mybir.AluOpType
AX = mybir.AxisListType

NB = 2
L = 2048
D = 1024
DI = 2048
NS = 16
DTR = 64
DFF = 4096
NE = 8
DH = 2048
TOK = NB * L
LC = 256
NCH = L // LC
QD = 256
NQ = L // QD
NDT = DI // 128
NDM = D // 128
LN_EPS = 1e-5
RMS_EPS = 1e-6


class KB:
    def __init__(self, nc, ctx):
        self.nc = nc
        self.ctx = ctx
        sem = lambda name: ctx.enter_context(nc.semaphore(name))
        self.Vs, self.Vn = sem("V"), 0
        self.Ss, self.Sn = sem("S"), 0
        self.Gs, self.Gn = sem("G"), 0
        self.PEs, self.PEn = sem("PE"), 0
        self.dma_sems = [sem("dma%d" % i) for i in range(4)]
        self.dma_cnt = [0, 0, 0, 0]
        self.dma_rr = 0
        self.gq_sems = {}
        self.banks = {}

    def bank_wait(self, eng, key):
        for sem, val in self.banks.get(key, []):
            eng.wait_ge(sem, val)

    def bank_set(self, key, toks):
        self.banks[key] = toks

    def sb(self, name, shape, dt=F32):
        return self.ctx.enter_context(self.nc.sbuf_tensor(name, shape, dt))

    def psum(self, name, shape, dt=F32):
        return self.ctx.enter_context(self.nc.psum_tensor(name, shape, dt))

    def v(self, instr):
        if self.Vn > 0:
            instr.wait_op(self.Vs, self.Vn, "sem-ge")
        instr.then_inc(self.Vs, 1)
        self.Vn += 1
        return self.Vn

    def s(self, instr):
        if self.Sn > 0:
            instr.wait_op(self.Ss, self.Sn, "sem-ge")
        instr.then_inc(self.Ss, 1)
        self.Sn += 1
        return self.Sn

    def g(self, instr):
        if self.Gn > 0:
            instr.wait_op(self.Gs, self.Gn, "sem-ge")
        instr.then_inc(self.Gs, 1)
        self.Gn += 1
        return self.Gn

    def pe_inc(self, instr):
        instr.then_inc(self.PEs, 1)
        self.PEn += 1
        return self.PEn

    def mmgrp(self, out_ap, ops):
        n = len(ops)
        last = None
        for i, (a, b) in enumerate(ops):
            last = self.nc.tensor.matmul(out_ap, a, b, start=(i == 0), stop=(i == n - 1))
        return self.pe_inc(last)

    def _dma(self, out, in_, transpose):
        k = self.dma_rr
        self.dma_rr = (self.dma_rr + 1) % 4
        s = self.dma_sems[k]
        c = self.dma_cnt[k]
        f = self.nc.sync.dma_start_transpose if transpose else self.nc.sync.dma_start
        instr = f(out, in_)
        if c > 0:
            instr.wait_op(s, 16 * c, "sem-ge")
        instr.then_inc(s, 16)
        self.dma_cnt[k] = c + 1
        return (s, 16 * (c + 1))

    def sydma(self, out, in_):
        return self._dma(out, in_, False)

    def trdma(self, out, in_):
        return self._dma(out, in_, True)

    def dma_mark(self):
        return [(s, 16 * c) for s, c in zip(self.dma_sems, self.dma_cnt) if c > 0]

    def wait_tok(self, eng, tok):
        if isinstance(tok, list):
            for s, v in tok:
                eng.wait_ge(s, v)
        else:
            eng.wait_ge(tok[0], tok[1])

    def gqdma(self, group, out, in_):
        if group not in self.gq_sems:
            self.gq_sems[group] = [self.ctx.enter_context(self.nc.semaphore("gq_" + group)), 0]
        s, c = self.gq_sems[group]
        instr = self.nc.gpsimd.dma_start(out, in_)
        if c > 0:
            instr.wait_op(s, c, "sem-ge")
        instr.then_inc(s, 16)
        self.gq_sems[group][1] = c + 16
        return self.gq_sems[group][1]

    def wait_gq(self, eng, group):
        s, n = self.gq_sems[group]
        if n > 0:
            eng.wait_ge(s, n)

    def barrier(self):
        nc = self.nc
        targets = [(self.Vs, self.Vn), (self.Ss, self.Sn), (self.Gs, self.Gn),
                   (self.PEs, self.PEn)]
        targets += [(s, 16 * c) for s, c in zip(self.dma_sems, self.dma_cnt)]
        for eng in (nc.vector, nc.scalar, nc.gpsimd, nc.tensor, nc.sync):
            for sem, val in targets:
                if val > 0:
                    eng.wait_ge(sem, val)


def dv(t, tile, n, lo=None, hi=None):
    if lo is None:
        return t[:, tile * n:(tile + 1) * n]
    return t[:, tile * n + lo: tile * n + hi]



def _build_setup(nc, kb, w, wbf):
    S = {}
    for group, names in [("a", ["in_proj", "x_proj", "dt_proj", "out_proj"]),
                         ("mlp", ["mlp_w1", "mlp_w2"]),
                         ("ffn", ["W1", "W2", "Wg", "b2"]),
                         ("wb", ["Wb"])]:
        for n in names:
            kb.gqdma(group, wbf[n][...], w[n][...])

    def vec_cols(name, dram, n):
        t = kb.sb(name, [128, n // 128], F32)
        view = dram.rearrange("(c p) -> c p", p=128)
        for j in range(n // 128):
            kb.sydma(t[:, j:j + 1], view[j:j + 1, :])
        return t

    S["convb"] = vec_cols("c_convb", w["conv_b"], DI)
    S["dtb"] = vec_cols("c_dtb", w["dt_b"], DI)
    S["Dvec"] = vec_cols("c_Dvec", w["Dv"], DI)
    S["n1w"] = vec_cols("c_n1w", w["norm1_w"], D)
    S["lng"] = vec_cols("c_lng", w["ln_g"], D)
    S["lnb"] = vec_cols("c_lnb", w["ln_b"], D)
    S["mlpb1"] = vec_cols("c_mlpb1", w["mlp_b1"], DFF)
    S["mlpb2"] = vec_cols("c_mlpb2", w["mlp_b2"], D)
    S["b1c"] = kb.sb("c_b1c", [128, NE * 16], F32)
    b1v = w["b1"].rearrange("e (c p) -> (e c) p", p=128)
    for j in range(NE * 16):
        kb.sydma(S["b1c"][:, j:j + 1], b1v[j:j + 1, :])

    S["convw"] = kb.sb("c_convw", [128, NDT * 4], F32)
    cw = w["conv_w"].rearrange("(t p) k -> t p k", p=128)
    for t in range(NDT):
        kb.sydma(S["convw"][:, t * 4:(t + 1) * 4], cw[t])

    S["Aneg"] = kb.sb("c_Aneg", [128, NDT * NS], F32)
    al = w["A_log"].rearrange("(t p) n -> p t n", p=128)
    t_alog = kb.sydma(S["Aneg"][:, :].rearrange("p (t n) -> p t n", n=NS), al)

    S["bg8"] = kb.sb("c_bg8", [8, 1], F32)
    kb.sydma(S["bg8"][:, :], w["bg"][:, :])
    S["eta1"] = kb.sb("c_eta1", [1, 1], F32)
    S["laml"] = kb.sb("c_laml", [1, 1], F32)
    t_eta = kb.sydma(S["eta1"][:, :], w["eta"][:, :])
    t_scal = kb.sydma(S["laml"][:, :], w["lam"][:, :])

    S["id_f"] = kb.sb("c_idf", [128, 128], F32)
    S["id_bf"] = kb.sb("c_idbf", [128, 128], BF16)
    S["ones_row_f"] = kb.sb("c_or_f", [1, 128], F32)
    S["ones_col_f"] = kb.sb("c_oc_f", [128, 1], F32)
    S["ones_col_bf"] = kb.sb("c_oc_bf", [128, 1], BF16)
    S["sel32"] = kb.sb("c_sel32", [32, 32 * 128], BF16)
    S["Upre0"] = kb.sb("c_Upre0", [128, 256], F32)
    S["Upre1"] = kb.sb("c_Upre1", [128, 256], F32)
    S["llog_c"] = kb.sb("c_llog_c", [128, 1], F32)
    S["beta_c"] = kb.sb("c_beta_c", [128, 1], F32)
    S["lamQ_c"] = kb.sb("c_lamQ_c", [128, 1], F32)
    S["nlamj1"] = kb.sb("c_nlamj1", [128, 2], F32)
    S["brev"] = kb.sb("c_brev", [128, 2], F32)
    S["epsln"] = kb.sb("c_epsln", [1, 1], F32)
    S["epsrms"] = kb.sb("c_epsrms", [1, 1], F32)

    tmp_ctx = ExitStack()
    _outer_ctx, kb.ctx = kb.ctx, tmp_ctx
    ii = kb.sb("c_ii", [128, 256], I32)
    iif = kb.sb("c_iif", [128, 256], F32)
    kb.g(nc.gpsimd.iota(ii[:, 0:128], [[1, 128]], base=0, channel_multiplier=-1))
    nc.vector.wait_ge(kb.Gs, kb.Gn)
    kb.v(nc.vector.tensor_scalar(S["id_f"][:, :], ii[:, 0:128], 0.0, None, ALU.is_equal))
    kb.v(nc.vector.tensor_copy(S["id_bf"][:, :], S["id_f"][:, :]))
    kb.v(nc.vector.memset(S["ones_row_f"][:, :], 1.0))
    kb.v(nc.vector.memset(S["ones_col_f"][:, :], 1.0))
    kb.v(nc.vector.memset(S["ones_col_bf"][:, :], 1.0))
    kb.v(nc.vector.memset(S["epsln"][:, :], LN_EPS))
    kb.v(nc.vector.memset(S["epsrms"][:, :], RMS_EPS))

    sel_i = kb.sb("c_sel_i", [32, 32 * 128], I32)
    kb.g(nc.gpsimd.iota(sel_i[:, :], [[-1, 32], [0, 128]], base=0, channel_multiplier=1))
    nc.vector.wait_ge(kb.Gs, kb.Gn)
    kb.v(nc.vector.tensor_scalar(S["sel32"][:, :], sel_i[:, :], 0.0, None, ALU.is_equal))

    kb.wait_tok(nc.scalar, t_alog)
    kb.s(nc.scalar.activation(S["Aneg"][:, :], S["Aneg"][:, :], AF.Exp))
    nc.vector.wait_ge(kb.Ss, kb.Sn)
    kb.v(nc.vector.tensor_scalar_mul(S["Aneg"][:, :], S["Aneg"][:, :], -1.0))

    lam1 = kb.sb("c_lam1", [1, 1], F32)
    llog1 = kb.sb("c_llog1", [1, 1], F32)
    beta1 = kb.sb("c_beta1", [1, 1], F32)
    lamQ1 = kb.sb("c_lamQ1", [1, 1], F32)
    kb.wait_tok(nc.scalar, t_scal)
    kb.wait_tok(nc.vector, t_eta)
    kb.s(nc.scalar.activation(lam1[:, :], S["laml"][:, :], AF.Sigmoid))
    kb.s(nc.scalar.activation(llog1[:, :], lam1[:, :], AF.Ln))
    nc.vector.wait_ge(kb.Ss, kb.Sn)
    kb.v(nc.vector.tensor_scalar(beta1[:, :], lam1[:, :], -1.0, 1.0, ALU.mult, ALU.add))
    kb.v(nc.vector.tensor_mul(beta1[:, :], beta1[:, :], S["eta1"][:, :]))
    v_beta = kb.v(nc.vector.tensor_scalar_mul(beta1[:, :], beta1[:, :], 1.0 / 64.0))
    nc.scalar.wait_ge(kb.Vs, v_beta)
    s_lamq = kb.s(nc.scalar.activation(lamQ1[:, :], llog1[:, :], AF.Exp, scale=float(QD)))

    pscal = kb.psum("c_pscal", [128, 4], F32)
    nc.tensor.wait_ge(kb.Ss, s_lamq)
    nc.tensor.wait_ge(kb.Vs, v_beta)
    nc.tensor.matmul(pscal[:, 0:1], S["ones_row_f"][:, :], llog1[:, :], start=True, stop=True)
    nc.tensor.matmul(pscal[:, 1:2], S["ones_row_f"][:, :], beta1[:, :], start=True, stop=True)
    pe_sc = kb.pe_inc(nc.tensor.matmul(pscal[:, 2:3], S["ones_row_f"][:, :], lamQ1[:, :], start=True, stop=True))
    nc.vector.wait_ge(kb.PEs, pe_sc)
    kb.v(nc.vector.tensor_copy(S["llog_c"][:, :], pscal[:, 0:1]))
    kb.v(nc.vector.tensor_copy(S["beta_c"][:, :], pscal[:, 1:2]))
    kb.v(nc.vector.tensor_copy(S["lamQ_c"][:, :], pscal[:, 2:3]))

    di0 = kb.sb("c_di0", [128, 256], I32)
    di1 = kb.sb("c_di1", [128, 256], I32)
    kb.g(nc.gpsimd.iota(di0[:, :], [[1, 256]], base=0, channel_multiplier=-1))
    kb.g(nc.gpsimd.iota(di1[:, :], [[1, 256]], base=-128, channel_multiplier=-1))
    nc.vector.wait_ge(kb.Gs, kb.Gn)
    for dsrc, udst in [(di0, S["Upre0"]), (di1, S["Upre1"])]:
        v_i = kb.v(nc.vector.tensor_copy(iif[:, :], dsrc[:, :]))
        nc.scalar.wait_ge(kb.Vs, v_i)
        s_e = kb.s(nc.scalar.activation(udst[:, :], iif[:, :], AF.Exp, scale=S["llog_c"][:, :]))
        nc.vector.wait_ge(kb.Ss, s_e)
        kb.v(nc.vector.tensor_scalar(iif[:, :], dsrc[:, :], 0.0, None, ALU.is_gt))
        kb.v(nc.vector.tensor_mul(udst[:, :], udst[:, :], iif[:, :]))
        kb.v(nc.vector.tensor_scalar(udst[:, :], udst[:, :], S["beta_c"][:, :], None, ALU.mult))

    ci = kb.sb("c_ci", [128, 1], I32)
    cf = kb.sb("c_cf", [128, 1], F32)
    for st in range(2):
        nc.gpsimd.wait_ge(kb.Vs, kb.Vn)
        kb.g(nc.gpsimd.iota(ci[:, :], [[0, 1]], base=1 + 128 * st, channel_multiplier=1))
        nc.vector.wait_ge(kb.Gs, kb.Gn)
        v_c = kb.v(nc.vector.tensor_copy(cf[:, :], ci[:, :]))
        nc.scalar.wait_ge(kb.Vs, v_c)
        s_c = kb.s(nc.scalar.activation(S["nlamj1"][:, st:st + 1], cf[:, :], AF.Exp, scale=S["llog_c"][:, :]))
        nc.vector.wait_ge(kb.Ss, s_c)
        kb.v(nc.vector.tensor_scalar_mul(S["nlamj1"][:, st:st + 1], S["nlamj1"][:, st:st + 1], -1.0))
        nc.gpsimd.wait_ge(kb.Vs, kb.Vn)
        kb.g(nc.gpsimd.iota(ci[:, :], [[0, 1]], base=QD - 1 - 128 * st, channel_multiplier=-1))
        nc.vector.wait_ge(kb.Gs, kb.Gn)
        v_c = kb.v(nc.vector.tensor_copy(cf[:, :], ci[:, :]))
        nc.scalar.wait_ge(kb.Vs, v_c)
        s_c = kb.s(nc.scalar.activation(S["brev"][:, st:st + 1], cf[:, :], AF.Exp, scale=S["llog_c"][:, :]))
        nc.vector.wait_ge(kb.Ss, s_c)
        kb.v(nc.vector.tensor_scalar(S["brev"][:, st:st + 1], S["brev"][:, st:st + 1], S["beta_c"][:, :], None, ALU.mult))
    kb.ctx = _outer_ctx
    kb.barrier()
    tmp_ctx.close()
    return S



def _stats(nc, kb, S, src, sq, stat, ps_sum, ps_ssq, ps_bmu, ps_brs,
           ntiles, n, eps_ap, rms, v_src, src_tiles=None):
    inv = 1.0 / (ntiles * 128)
    tiles = src_tiles if src_tiles is not None else [dv(src, t, n) for t in range(ntiles)]
    nc.tensor.wait_ge(kb.Vs, kb.Vn)
    nc.scalar.wait_ge(kb.Vs, v_src)
    for t in range(ntiles):
        kb.s(nc.scalar.activation(dv(sq, t, n), tiles[t], AF.Square))
    s_sq = kb.Sn
    nc.tensor.wait_ge(kb.Ss, s_sq)
    if not rms:
        nc.tensor.wait_ge(kb.Vs, v_src)
        kb.mmgrp(ps_sum, [(S["ones_col_f"][:, :], tiles[t]) for t in range(ntiles)])
    ones_sq = S["ones_col_bf"] if sq.dtype == BF16 else S["ones_col_f"]
    pe_ssq = kb.mmgrp(ps_ssq, [(ones_sq[:, :], dv(sq, t, n)) for t in range(ntiles)])
    mu = stat[0:1, 0:n]
    var = stat[0:1, n:2 * n]
    m2 = stat[0:1, 2 * n:3 * n]
    nc.vector.wait_ge(kb.PEs, pe_ssq)
    if rms:
        v_var = kb.v(nc.vector.tensor_scalar_mul(var, ps_ssq, inv))
    else:
        kb.v(nc.vector.tensor_scalar_mul(mu, ps_sum, inv))
        kb.v(nc.vector.tensor_mul(m2, mu, mu))
        v_var = kb.v(nc.vector.scalar_tensor_tensor(var, ps_ssq, inv, m2,
                                                    ALU.mult, ALU.subtract))
    nc.scalar.wait_ge(kb.Vs, v_var)
    kb.s(nc.scalar.activation(m2, var, AF.Ln, bias=eps_ap))
    s_r = kb.s(nc.scalar.activation(var, m2, AF.Exp, scale=-0.5))
    nc.tensor.wait_ge(kb.Ss, s_r)
    nc.tensor.wait_ge(kb.Vs, kb.Vn)
    mu_bc = None
    if not rms:
        kb.mmgrp(ps_bmu, [(S["ones_row_f"][:, :], mu)])
        mu_bc = ps_bmu
    pe_bc = kb.mmgrp(ps_brs, [(S["ones_row_f"][:, :], var)])
    return pe_bc, mu_bc, ps_brs



def _phase1(nc, kb, S, xin, w, wbf, xd_hbm, zm_hbm):
    with ExitStack() as ctx:
        kb_ctx, kb.ctx = kb.ctx, ctx
        try:
            _phase1_body(nc, kb, S, xin, w, wbf, xd_hbm, zm_hbm)
        finally:
            kb.barrier()
            kb.ctx = kb_ctx


def _phase1_body(nc, kb, S, xin, w, wbf, xd_hbm, zm_hbm):
    wTin = kb.sb("p1_wTin", [128, 8 * (2 * DI)], BF16)
    wTxp = kb.sb("p1_wTxp", [128, NDT * 96], BF16)
    dtp_n = kb.sb("p1_dtp_n", [128, NDT * DTR], BF16)
    wTdt = kb.sb("p1_wTdt", [64, NDT * 128], BF16)

    kb.wait_gq(nc.sync, "a")
    for kt in range(8):
        kb.trdma(wTin[:, kt * 4096:(kt + 1) * 4096], wbf["in_proj"][:, kt * 128:(kt + 1) * 128])
    for kt in range(NDT):
        kb.trdma(dv(wTxp, kt, 96), wbf["x_proj"][:, kt * 128:(kt + 1) * 128])
    t_dtp = kb.sydma(dtp_n[:, :].rearrange("p (t r) -> p t r", r=DTR),
                     wbf["dt_proj"].rearrange("(t p) r -> p t r", p=128))
    t_wts = kb.dma_mark()

    p_tr = kb.psum("p1_ptr", [64, 256], BF16)
    kb.wait_tok(nc.tensor, t_dtp)
    for kt in range(NDT):
        half = kt % 2
        nc.tensor.wait_ge(kb.Vs, kb.Vn)
        pe_t = kb.pe_inc(nc.tensor.transpose(p_tr[:, half * 128:(half + 1) * 128],
                                             dv(dtp_n, kt, DTR), S["id_bf"][:, :]))
        nc.vector.wait_ge(kb.PEs, pe_t)
        kb.v(nc.vector.tensor_copy(dv(wTdt, kt, 128), p_tr[:, half * 128:(half + 1) * 128]))

    xd = kb.sb("p1_xd", [128, NDM * LC], F32)
    sq = kb.sb("p1_sq", [128, NDM * LC], BF16)
    xstage = kb.sb("p1_xstage", [128, 2048], F32)
    zln = kb.sb("p1_zln", [128, NDM * LC], BF16)
    tmp8 = kb.sb("p1_tmp8", [128, LC], F32)
    cvt = kb.sb("p1_cvt", [128, 2 * LC], F32)
    stat = kb.sb("p1_stat", [1, 3 * LC], F32)
    xcin = kb.sb("p1_xcin", [128, NDT * (LC + 4)], BF16)
    z_sil = kb.sb("p1_zsil", [128, NDT * LC], BF16)
    xc_bf = kb.sb("p1_xcbf", [128, NDT * LC], BF16)
    dtin_bf = kb.sb("p1_dtin", [64, LC], BF16)
    BC_bf = kb.sb("p1_BCbf", [32, LC], BF16)
    bcBC = kb.sb("p1_bcBC", [128, 32 * LC], BF16)
    sp1 = kb.sb("p1_sp1", [128, LC], F32)
    sp2 = kb.sb("p1_sp2", [128, LC], F32)
    u_t = kb.sb("p1_ut", [128, LC], F32)
    LCP = LC + 1
    a16 = kb.sb("p1_a16", [128, NS * LCP], BF16)
    b16 = kb.sb("p1_b16", [128, NS * LCP], BF16)
    h16 = kb.sb("p1_h16", [128, NS * LCP], BF16)
    wring = kb.sb("p1_wring", [128, 8 * 512], BF16)
    dt_t = sp1
    yacc = kb.sb("p1_yacc", [128, LC], F32)
    ygate = kb.sb("p1_ygate", [128, NDT * LC], BF16)
    state = kb.sb("p1_state", [128, NDT * NS], F32)
    zm_bf = kb.sb("p1_zmbf", [128, NDM * LC], BF16)

    p_in = kb.psum("p1_pin", [128, 512], F32)
    p_in2 = kb.psum("p1_pin2", [128, 512], F32)
    p_xdbl = kb.psum("p1_pxdbl", [128, LC], F32)
    p_stat = kb.psum("p1_pstat", [1, 2 * LC], F32)
    p_bc = kb.psum("p1_pbc", [128, 2 * LC], F32)
    p_sel = kb.psum("p1_psel", [128, 256], F32)
    p_sel2 = kb.psum("p1_psel2", [128, 256], F32)

    xin_f = xin.rearrange("b l d -> (b l) d")
    xd_view = xd_hbm.rearrange("(t p) T -> p t T", p=128)
    zm_view = zm_hbm.rearrange("(t p) T -> p t T", p=128)

    kb.v(nc.vector.memset(a16[:, :], 0.0))
    for b in range(NB):
        kb.v(nc.vector.memset(state[:, :], 0.0))
        for ci in range(NCH):
            toff = b * L + ci * LC
            nc.vector.wait_ge(kb.Ss, kb.Sn)
            for dt in range(NDT):
                if ci == 0:
                    kb.v(nc.vector.memset(dv(xcin, dt, LC + 4, 0, 3), 0.0))
                else:
                    kb.v(nc.vector.tensor_copy(dv(xcin, dt, LC + 4, 0, 3),
                                               dv(xcin, dt, LC + 4, LC, LC + 3)))
            kb.bank_wait(nc.sync, "xstg")
            for tt in range(2):
                kb.sydma(xstage[:, tt * 1024:(tt + 1) * 1024],
                         xin_f[toff + tt * 128: toff + (tt + 1) * 1024 // 1024 * 128, :])
            t_x = kb.dma_mark()
            kb.wait_tok(nc.tensor, t_x)
            if ci == 0 and b == 0:
                kb.wait_tok(nc.tensor, t_wts)
            kb.bank_wait(nc.vector, "xdbuf")
            for grp in range(4):
                pp = (p_in if grp % 2 == 0 else p_in2)
                bk = "p1p%d" % (grp % 2)
                kb.bank_wait(nc.tensor, bk)
                pe_t = 0
                for q in range(4):
                    k = grp * 4 + q
                    ds, tt = k // 2, k % 2
                    pe_t = kb.pe_inc(nc.tensor.transpose(
                        pp[:, q * 128:(q + 1) * 128],
                        xstage[:, tt * 1024 + ds * 128: tt * 1024 + (ds + 1) * 128],
                        S["id_f"][:, :]))
                nc.vector.wait_ge(kb.PEs, pe_t)
                v_e = kb.v(nc.vector.tensor_copy(xd[:, grp * 512:(grp + 1) * 512],
                                                 pp[:, 0:512]))
                kb.bank_set(bk, [(kb.Vs, v_e)])
                if grp == 3:
                    kb.bank_set("xstg", [(kb.PEs, kb.PEn)])
            v_xd = kb.Vn
            nc.sync.wait_ge(kb.Vs, v_xd)
            t_xd = kb.sydma(xd_view[:, :, toff:toff + LC],
                            xd[:, :].rearrange("p (t n) -> p t n", n=LC))

            pe_bc, mu_bc, rs_bc = _stats(nc, kb, S, xd, sq, stat,
                                         p_stat[0:1, 0:LC], p_stat[0:1, LC:2 * LC],
                                         p_bc[:, 0:LC], p_bc[:, LC:2 * LC],
                                         NDM, LC, S["epsln"][:, :], False, v_xd)
            nc.vector.wait_ge(kb.PEs, pe_bc)
            kb.bank_wait(nc.vector, "zlnbuf")
            for t in range(NDM):
                kb.v(nc.vector.tensor_sub(tmp8[:, :], dv(xd, t, LC), mu_bc))
                kb.v(nc.vector.tensor_mul(dv(zln, t, LC), tmp8[:, :], rs_bc))
            v_zln = kb.Vn

            nc.tensor.wait_ge(kb.Vs, v_zln)
            kb.bank_wait(nc.scalar, "xzsink")
            for m in range(32):
                pp = (p_in if m % 2 == 0 else p_in2)
                bk = "p1p%d" % (m % 2)
                kb.bank_wait(nc.tensor, bk)
                pe_m = kb.mmgrp(pp[:, 0:LC],
                                [(wTin[:, kt * 4096 + m * 128:kt * 4096 + (m + 1) * 128],
                                  dv(zln, kt, LC)) for kt in range(8)])
                if m < 16:
                    nc.scalar.wait_ge(kb.PEs, pe_m)
                    s_e = kb.s(nc.scalar.activation(dv(xcin, m, LC + 4, 3, LC + 3),
                                                    pp[:, 0:LC], AF.Identity))
                    kb.bank_set(bk, [(kb.Ss, s_e)])
                else:
                    nc.scalar.wait_ge(kb.PEs, pe_m)
                    s_e = kb.s(nc.scalar.activation(dv(z_sil, m - 16, LC), pp[:, 0:LC], AF.Silu))
                    kb.bank_set(bk, [(kb.Ss, s_e)])

            kb.bank_set("zlnbuf", [(kb.PEs, kb.PEn)])
            nc.vector.wait_ge(kb.Ss, kb.Sn)
            kb.bank_wait(nc.scalar, "xcbf")
            silu_tok = [0, 0]
            for dt in range(NDT):
                cv = cvt[:, (dt % 2) * LC:(dt % 2 + 1) * LC]
                if silu_tok[dt % 2]:
                    nc.vector.wait_ge(kb.Ss, silu_tok[dt % 2])
                kb.v(nc.vector.tensor_scalar(cv, dv(xcin, dt, LC + 4, 0, LC),
                                             S["convw"][:, dt * 4:dt * 4 + 1], None, ALU.mult))
                for kk in range(1, 4):
                    kb.v(nc.vector.scalar_tensor_tensor(
                        cv, dv(xcin, dt, LC + 4, kk, LC + kk),
                        S["convw"][:, dt * 4 + kk:dt * 4 + kk + 1], cv,
                        ALU.mult, ALU.add))
                v_cv = kb.Vn
                nc.scalar.wait_ge(kb.Vs, v_cv)
                silu_tok[dt % 2] = kb.s(nc.scalar.activation(
                    dv(xc_bf, dt, LC), cv, AF.Silu, bias=S["convb"][:, dt:dt + 1]))
            s_xc = kb.Sn

            nc.tensor.wait_ge(kb.Ss, s_xc)
            kb.bank_wait(nc.tensor, "p1x")
            kb.bank_wait(nc.tensor, "dtinb")
            pe_xd = kb.mmgrp(p_xdbl[0:96, :],
                             [(dv(wTxp, kt, 96), dv(xc_bf, kt, LC)) for kt in range(NDT)])
            nc.vector.wait_ge(kb.PEs, pe_xd)
            kb.v(nc.vector.tensor_copy(dtin_bf[:, :], p_xdbl[0:64, :]))
            v_bc0 = kb.v(nc.vector.tensor_copy(BC_bf[:, :], p_xdbl[64:96, :]))
            kb.bank_set("p1x", [(kb.Vs, v_bc0)])
            kb.bank_set("xcbf", [(kb.PEs, kb.PEn)])

            nc.tensor.wait_ge(kb.Vs, v_bc0)
            kb.bank_wait(nc.scalar, "bcsink")
            for j in range(32):
                pp = (p_sel if j % 2 == 0 else p_sel2)[:, 0:LC]
                bk = "p1s%d" % (j % 2)
                kb.bank_wait(nc.tensor, bk)
                pe_j = kb.mmgrp(pp, [(S["sel32"][:, j * 128:(j + 1) * 128], BC_bf[:, :])])
                nc.scalar.wait_ge(kb.PEs, pe_j)
                s_e = kb.s(nc.scalar.activation(dv(bcBC, j, LC), pp, AF.Identity))
                kb.bank_set(bk, [(kb.Ss, s_e)])

            s_bc_done = kb.Sn
            nc.tensor.wait_ge(kb.Vs, v_bc0)
            for dt in range(NDT):
                pp = (p_in if dt % 2 == 0 else p_in2)
                bk = "p1p%d" % (dt % 2)
                kb.bank_wait(nc.tensor, bk)
                pe_d = kb.mmgrp(pp[:, 0:LC], [(dv(wTdt, dt, 128), dtin_bf[:, :])])
                nc.scalar.wait_ge(kb.PEs, pe_d)
                dtb_c = S["dtb"][:, dt:dt + 1]
                kb.s(nc.scalar.activation(sp1[:, :], pp[:, 0:LC], AF.Relu, bias=dtb_c))
                s_sp = kb.s(nc.scalar.activation(sp2[:, :], pp[:, 0:LC], AF.Abs, bias=dtb_c))
                kb.bank_set(bk, [(kb.Ss, s_sp)])
                kb.s(nc.scalar.activation(sp2[:, :], sp2[:, :], AF.Exp, scale=-1.0))
                s_sp = kb.s(nc.scalar.activation(sp2[:, :], sp2[:, :], AF.Ln, bias=S["ones_col_f"][:, :]))
                nc.gpsimd.wait_ge(kb.Ss, s_sp)
                nc.gpsimd.wait_ge(kb.Vs, kb.Vn)
                if dt == 0:
                    nc.vector.wait_ge(kb.Ss, s_bc_done)
                kb.g(nc.gpsimd.tensor_add(dt_t[:, :], sp1[:, :], sp2[:, :]))
                g_dt = kb.g(nc.gpsimd.tensor_mul(u_t[:, :], dt_t[:, :], dv(xc_bf, dt, LC)))
                nc.scalar.wait_ge(kb.Gs, g_dt)
                for n in range(NS):
                    kb.s(nc.scalar.activation(
                        a16[:, n * LCP + 1:(n + 1) * LCP], dt_t[:, :], AF.Exp,
                        scale=S["Aneg"][:, dt * NS + n:dt * NS + n + 1]))
                s_a = kb.Sn
                breal = b16[:, :].rearrange("p (n lp) -> p n lp", lp=LCP)[:, :, 1:LCP]
                hreal = h16[:, :].rearrange("p (n lp) -> p n lp", lp=LCP)[:, :, 1:LCP]
                g_tok = []
                for q in range(2):
                    g_tok.append(kb.g(nc.gpsimd.tensor_mul(
                        breal[:, q * 8:(q + 1) * 8, :],
                        u_t[:, :].unsqueeze(1).broadcast_to([128, 8, LC]),
                        bcBC[:, q * 8 * LC:(q + 1) * 8 * LC].rearrange(
                            "p (n l) -> p n l", n=8))))
                kb.v(nc.vector.tensor_copy(b16[:, 0:NS * LCP:LCP],
                                           state[:, dt * NS:(dt + 1) * NS]))
                nc.vector.wait_ge(kb.Ss, s_a)
                nc.vector.wait_ge(kb.Gs, g_tok[-1])
                kb.v(nc.vector.tensor_tensor_scan(
                    h16[:, :], a16[:, :], b16[:, :], 0.0, ALU.mult, ALU.add))
                kb.v(nc.vector.tensor_copy(state[:, dt * NS:(dt + 1) * NS],
                                           h16[:, LC:NS * LCP:LCP]))
                kb.v(nc.vector.tensor_mul(
                    breal, hreal,
                    bcBC[:, NS * LC:2 * NS * LC].rearrange("p (n l) -> p n l", n=NS)))
                kb.v(nc.vector.tensor_add(breal[:, 0:8, :], breal[:, 0:8, :],
                                          breal[:, 8:16, :]))
                kb.v(nc.vector.tensor_add(breal[:, 0:4, :], breal[:, 0:4, :],
                                          breal[:, 4:8, :]))
                kb.v(nc.vector.tensor_add(breal[:, 0:2, :], breal[:, 0:2, :],
                                          breal[:, 2:4, :]))
                kb.v(nc.vector.tensor_add(yacc[:, :], breal[:, 0, :], breal[:, 1, :]))
                if dt == 0:
                    kb.bank_wait(nc.vector, "ygateb")
                kb.v(nc.vector.scalar_tensor_tensor(
                    yacc[:, :], dv(xc_bf, dt, LC), S["Dvec"][:, dt:dt + 1], yacc[:, :],
                    ALU.mult, ALU.add))
                kb.v(nc.vector.tensor_mul(dv(ygate, dt, LC), yacc[:, :], dv(z_sil, dt, LC)))
            v_yg = kb.Vn
            kb.bank_set("xzsink", [(kb.Vs, v_yg)])
            kb.bank_set("bcsink", [(kb.Gs, kb.Gn), (kb.Vs, v_yg)])
            kb.bank_set("dtinb", [(kb.PEs, kb.PEn)])

            kb.wait_tok(nc.vector, t_xd)
            nc.tensor.wait_ge(kb.Vs, v_yg)
            banks4 = [p_in[:, 0:LC], p_in2[:, 0:LC], p_xdbl[:, 0:LC], p_sel[:, 0:LC]]
            bkeys4 = ["p1p0", "p1p1", "p1x", "p1s0"]
            for mp in range(2):
                for bk in bkeys4:
                    kb.bank_wait(nc.tensor, bk)
                for kt in range(NDT):
                    r = kt % 8
                    kb.bank_wait(nc.sync, "p1wr%d" % r)
                    tok = kb.trdma(wring[:, r * 512:(r + 1) * 512],
                                   wbf["out_proj"][mp * 512:(mp + 1) * 512,
                                                   kt * 128:(kt + 1) * 128])
                    kb.wait_tok(nc.tensor, tok)
                    for mi in range(4):
                        last = nc.tensor.matmul(
                            banks4[mi], wring[:, r * 512 + mi * 128:r * 512 + (mi + 1) * 128],
                            dv(ygate, kt, LC), start=(kt == 0), stop=(kt == NDT - 1))
                    kb.bank_set("p1wr%d" % r, [(kb.PEs, kb.pe_inc(last))])
                nc.vector.wait_ge(kb.PEs, kb.PEn)
                for mi in range(4):
                    m = mp * 4 + mi
                    v_e = kb.v(nc.vector.tensor_add(dv(xd, m, LC), banks4[mi], dv(zln, m, LC)))
                    kb.bank_set(bkeys4[mi], [(kb.Vs, v_e)])
            v_m1 = kb.Vn
            kb.bank_set("ygateb", [(kb.PEs, kb.PEn)])

            pe_bc2, _, rs2 = _stats(nc, kb, S, xd, sq, stat,
                                    p_stat[0:1, 0:LC], p_stat[0:1, LC:2 * LC],
                                    p_bc[:, 0:LC], p_bc[:, LC:2 * LC],
                                    NDM, LC, S["epsrms"][:, :], True, v_m1)
            nc.vector.wait_ge(kb.PEs, pe_bc2)
            kb.bank_wait(nc.vector, "zmstore")
            for t in range(NDM):
                kb.v(nc.vector.tensor_mul(tmp8[:, :], dv(xd, t, LC), rs2))
                kb.v(nc.vector.tensor_scalar(dv(zm_bf, t, LC), tmp8[:, :],
                                             S["n1w"][:, t:t + 1], None, ALU.mult))
            v_zm = kb.Vn
            kb.bank_set("xdbuf", [(kb.PEs, kb.PEn)])
            nc.sync.wait_ge(kb.Vs, v_zm)
            t_zm = kb.sydma(zm_view[:, :, toff:toff + LC],
                            zm_bf[:, :].rearrange("p (t n) -> p t n", n=LC))
            kb.bank_set("zmstore", [t_zm])



def _phase2(nc, kb, S, w, wbf, xd_hbm, zm_hbm, x1_hbm):
    with ExitStack() as ctx:
        kb_ctx, kb.ctx = kb.ctx, ctx
        try:
            _phase2_body(nc, kb, S, wbf, xd_hbm, zm_hbm, x1_hbm)
        finally:
            kb.barrier()
            kb.ctx = kb_ctx


def _phase2_body(nc, kb, S, wbf, xd_hbm, zm_hbm, x1_hbm):
    TS = 256
    NSB = TOK // TS
    wT1 = kb.sb("p2_wT1", [128, 8 * DFF], BF16)
    wT2 = kb.sb("p2_wT2", [128, 32 * D], BF16)
    kb.wait_gq(nc.sync, "mlp")
    for kt in range(8):
        kb.trdma(wT1[:, kt * DFF:(kt + 1) * DFF], wbf["mlp_w1"][:, kt * 128:(kt + 1) * 128])
    for kt in range(32):
        kb.trdma(wT2[:, kt * D:(kt + 1) * D], wbf["mlp_w2"][:, kt * 128:(kt + 1) * 128])
    t_wts = kb.dma_mark()

    zm_sub = kb.sb("p2_zm", [128, NDM * TS], BF16)
    xr_sub = kb.sb("p2_xr", [128, NDM * TS], F32)
    h1 = kb.sb("p2_h1", [128, 32 * TS], BF16)
    m1 = kb.sb("p2_m1", [128, NDM * TS], F32)
    sq = kb.sb("p2_sq", [128, NDM * TS], F32)
    stat = kb.sb("p2_stat", [1, 3 * TS], F32)
    tmp = kb.sb("p2_tmp", [128, TS], F32)
    pb = [kb.psum("p2_pb%d" % i, [128, 256], F32) for i in range(8)]

    xd_view = xd_hbm.rearrange("(t p) T -> p t T", p=128)
    zm_view = zm_hbm.rearrange("(t p) T -> p t T", p=128)
    x1_view = x1_hbm.rearrange("(t p) T -> p t T", p=128)

    kb.wait_tok(nc.tensor, t_wts)
    for sub in range(NSB):
        toff = sub * TS
        kb.bank_wait(nc.sync, "p2ld")
        kb.sydma(zm_sub[:, :].rearrange("p (t n) -> p t n", n=TS),
                 zm_view[:, :, toff:toff + TS])
        kb.sydma(xr_sub[:, :].rearrange("p (t n) -> p t n", n=TS),
                 xd_view[:, :, toff:toff + TS])
        t_ld = kb.dma_mark()
        kb.wait_tok(nc.tensor, t_ld)
        for ht in range(32):
            pp = pb[ht % 2]
            bk = "p2b%d" % (ht % 2)
            kb.bank_wait(nc.tensor, bk)
            pe_h = kb.mmgrp(pp[:, :],
                            [(wT1[:, kt * DFF + ht * 128:kt * DFF + (ht + 1) * 128],
                              dv(zm_sub, kt, TS)) for kt in range(8)])
            nc.scalar.wait_ge(kb.PEs, pe_h)
            s_e = kb.s(nc.scalar.activation(dv(h1, ht, TS), pp[:, :], AF.Gelu,
                                            bias=S["mlpb1"][:, ht:ht + 1]))
            kb.bank_set(bk, [(kb.Ss, s_e)])
        s_h1 = kb.Sn
        nc.tensor.wait_ge(kb.Ss, s_h1)
        pes = []
        for m in range(NDM):
            kb.bank_wait(nc.tensor, "p2b%d" % m)
            pes.append(kb.mmgrp(pb[m][:, :],
                                [(wT2[:, kt * D + m * 128:kt * D + (m + 1) * 128],
                                  dv(h1, kt, TS)) for kt in range(32)]))
        for m in range(NDM):
            nc.vector.wait_ge(kb.PEs, pes[m])
            v_e = kb.v(nc.vector.scalar_tensor_tensor(
                dv(m1, m, TS), pb[m][:, :], S["mlpb2"][:, m:m + 1], dv(zm_sub, m, TS),
                ALU.add, ALU.add))
            kb.bank_set("p2b%d" % m, [(kb.Vs, v_e)])
        v_m1 = kb.Vn
        for i in range(4):
            kb.bank_wait(nc.tensor, "p2b%d" % i)
        pe_bc, _, rs = _stats(nc, kb, S, m1, sq, stat,
                              pb[0][0:1, :], pb[1][0:1, :], pb[2][:, :], pb[3][:, :],
                              NDM, TS, S["epsrms"][:, :], True, v_m1)
        nc.vector.wait_ge(kb.PEs, pe_bc)
        kb.wait_tok(nc.vector, t_ld)
        for m in range(NDM):
            kb.v(nc.vector.tensor_mul(tmp[:, :], dv(m1, m, TS), rs))
            kb.v(nc.vector.scalar_tensor_tensor(
                dv(xr_sub, m, TS), tmp[:, :], S["n1w"][:, m:m + 1], dv(xr_sub, m, TS),
                ALU.mult, ALU.add))
        v_x1 = kb.Vn
        for i in range(4):
            kb.bank_set("p2b%d" % i, [(kb.Vs, v_x1)])
        nc.sync.wait_ge(kb.Vs, v_x1)
        t_st = kb.sydma(x1_view[:, :, toff:toff + TS],
                        xr_sub[:, :].rearrange("p (t n) -> p t n", n=TS))
        kb.bank_set("p2ld", [(kb.PEs, kb.PEn), t_st])


def _phase_ffn(nc, kb, S, w, wbf, xs_hbm, xo_hbm, out_dram, tag):
    with ExitStack() as ctx:
        kb_ctx, kb.ctx = kb.ctx, ctx
        try:
            _ffn_body(nc, kb, S, wbf, xs_hbm, xo_hbm, out_dram, tag)
        finally:
            kb.barrier()
            kb.ctx = kb_ctx


def _ffn_body(nc, kb, S, wbf, xs_hbm, xo_hbm, out_dram, tag):
    TH = 1024
    CH = 128
    T5 = 512
    xacc = kb.sb(tag + "_xacc", [128, NDM * TH], F32)
    xln = kb.sb(tag + "_xln", [128, NDM * TH], BF16)
    w1eT = kb.sb(tag + "_w1eT", [128, 2 * 8 * DH], BF16)
    w2eT = kb.sb(tag + "_w2eT", [128, 16 * D], BF16)
    h1 = kb.sb(tag + "_h1", [128, 16 * T5], BF16)
    cbc = kb.sb(tag + "_cbc", [128, T5], BF16)
    hstage = kb.sb(tag + "_hstg", [128, 2 * T5], BF16)
    ostage = kb.sb(tag + "_ostg", [128, 384], F32) if out_dram is not None else None
    combT = kb.sb(tag + "_combT", [8, TH], BF16)
    g_sb = kb.sb(tag + "_gsb", [8, T5], BF16)
    gt = kb.sb(tag + "_gt", [128, 8], F32)
    gw = kb.sb(tag + "_gw", [128, 8], F32)
    cmb = kb.sb(tag + "_cmb", [128, 8], F32)
    cmb_bf = kb.sb(tag + "_cmbbf", [128, 8], BF16)
    col1 = kb.sb(tag + "_col1", [128, 4], F32)
    wg_sb = kb.sb(tag + "_wgsb", [8, D], BF16)
    wgT = kb.sb(tag + "_wgT", [128, 8 * 8], BF16)
    b2sb = kb.sb(tag + "_b2sb", [8, D], BF16)
    sq = kb.sb(tag + "_sq", [128, NDM * CH], BF16)
    stat = kb.sb(tag + "_stat", [1, 3 * CH], F32)
    tmp = kb.sb(tag + "_tmp", [128, CH], F32)

    pb = [kb.psum(tag + "_pb%d" % i, [128, 512], F32) for i in range(7)]
    ptr8 = kb.psum(tag + "_ptr8", [128, 128], BF16)
    ptrf = pb[6]

    xs_view = xs_hbm.rearrange("(t p) T -> p t T", p=128)
    if xo_hbm is not None:
        xo_view = xo_hbm.rearrange("(t p) T -> p t T", p=128)

    kb.wait_gq(nc.sync, "ffn")
    kb.sydma(wg_sb[:, :], wbf["Wg"][:, :])
    kb.sydma(b2sb[:, :], wbf["b2"][:, :])
    t_g = kb.dma_mark()
    kb.wait_tok(nc.tensor, t_g)
    for kt in range(8):
        pe_t = kb.pe_inc(nc.tensor.transpose(ptr8[:, (kt % 2) * 8:(kt % 2) * 8 + 8],
                                             wg_sb[:, kt * 128:(kt + 1) * 128],
                                             S["id_bf"][0:8, 0:8]))
        nc.vector.wait_ge(kb.PEs, pe_t)
        kb.v(nc.vector.tensor_copy(wgT[:, kt * 8:(kt + 1) * 8],
                                   ptr8[:, (kt % 2) * 8:(kt % 2) * 8 + 8]))
        nc.tensor.wait_ge(kb.Vs, kb.Vn)

    for half in range(TOK // TH):
        hoff = half * TH
        kb.bank_wait(nc.sync, tag + "qld")
        t_x = kb.sydma(xacc[:, :].rearrange("p (t n) -> p t n", n=TH),
                       xs_view[:, :, hoff:hoff + TH])
        kb.wait_tok(nc.vector, t_x)
        kb.bank_wait(nc.vector, tag + "qld")
        kb.bank_wait(nc.scalar, tag + "qld")
        v_x = kb.v(nc.vector.tensor_copy(xacc[:, 0:1], xacc[:, 0:1]))
        for c in range(TH // CH):
            tiles = [dv(xacc, t, TH, c * CH, (c + 1) * CH) for t in range(NDM)]
            pe_bc, mu_bc, rs_bc = _stats(nc, kb, S, None, sq, stat,
                                         pb[0][0:1, 0:CH], pb[1][0:1, 0:CH],
                                         pb[2][:, 0:CH], pb[3][:, 0:CH],
                                         NDM, CH, S["epsln"][:, :], False, v_x,
                                         src_tiles=tiles)
            nc.vector.wait_ge(kb.PEs, pe_bc)
            for t in range(NDM):
                kb.v(nc.vector.tensor_sub(tmp[:, :], tiles[t], mu_bc))
                kb.v(nc.vector.tensor_mul(dv(xln, t, TH, c * CH, (c + 1) * CH),
                                          tmp[:, :], rs_bc))
        v_xln = kb.Vn

        nc.tensor.wait_ge(kb.Vs, v_xln)
        for c5 in range(TH // T5):
            pe_g = kb.mmgrp(pb[4][0:8, 0:T5],
                            [(wgT[:, kt * 8:(kt + 1) * 8],
                              dv(xln, kt, TH, c5 * T5, (c5 + 1) * T5)) for kt in range(NDM)])
            nc.scalar.wait_ge(kb.PEs, pe_g)
            s_g = kb.s(nc.scalar.activation(g_sb[:, :], pb[4][0:8, 0:T5], AF.Identity,
                                            bias=S["bg8"][:, :]))
            for i in range(T5 // 128):
                nc.tensor.wait_ge(kb.Ss, s_g)
                nc.tensor.wait_ge(kb.Vs, kb.Vn)
                pe_t = kb.pe_inc(nc.tensor.transpose(
                    ptr8[:, 8:16], g_sb[:, i * 128:(i + 1) * 128], S["id_bf"][0:8, 0:8]))
                nc.vector.wait_ge(kb.PEs, pe_t)
                kb.v(nc.vector.tensor_copy(gt[:, :], ptr8[:, 8:16]))
                kb.v(nc.vector.reduce_max(col1[:, 0:1], gt[:, :], axis=AX.X))
                kb.v(nc.vector.tensor_scalar(cmb[:, :], gt[:, :], col1[:, 0:1], None, ALU.is_equal))
                kb.v(nc.vector.scalar_tensor_tensor(gw[:, :], cmb[:, :], -1e30, gt[:, :],
                                                    ALU.mult, ALU.add))
                kb.v(nc.vector.reduce_max(col1[:, 1:2], gw[:, :], axis=AX.X))
                kb.v(nc.vector.tensor_scalar(gw[:, :], gw[:, :], col1[:, 1:2], None, ALU.is_equal))
                v_d = kb.v(nc.vector.tensor_sub(col1[:, 2:3], col1[:, 0:1], col1[:, 1:2]))
                nc.scalar.wait_ge(kb.Vs, v_d)
                kb.s(nc.scalar.activation(col1[:, 3:4], col1[:, 2:3], AF.Sigmoid))
                s_s2 = kb.s(nc.scalar.activation(col1[:, 2:3], col1[:, 2:3], AF.Sigmoid,
                                                 scale=-1.0))
                nc.vector.wait_ge(kb.Ss, s_s2)
                kb.v(nc.vector.tensor_scalar(cmb[:, :], cmb[:, :], col1[:, 3:4], None, ALU.mult))
                kb.v(nc.vector.scalar_tensor_tensor(cmb[:, :], gw[:, :], col1[:, 2:3], cmb[:, :],
                                                    ALU.mult, ALU.add))
                v_c = kb.v(nc.vector.tensor_copy(cmb_bf[:, :], cmb[:, :]))
                nc.tensor.wait_ge(kb.Vs, v_c)
                pe_t2 = kb.pe_inc(nc.tensor.transpose(ptr8[0:8, 0:128], cmb_bf[:, :], S["id_bf"][:, :]))
                nc.vector.wait_ge(kb.PEs, pe_t2)
                kb.v(nc.vector.tensor_copy(combT[:, c5 * T5 + i * 128:c5 * T5 + (i + 1) * 128],
                                           ptr8[0:8, 0:128]))
        v_comb = kb.Vn

        def load_w1(e, slot):
            kb.bank_wait(nc.sync, tag + "w1s%d" % slot)
            base = slot * 8 * DH
            for kt in range(8):
                kb.trdma(w1eT[:, base + kt * DH:base + (kt + 1) * DH],
                         wbf["W1"][e][:, kt * 128:(kt + 1) * 128])
            return kb.dma_mark()

        def load_w2(e):
            kb.bank_wait(nc.sync, tag + "w2s")
            for kt in range(16):
                kb.trdma(w2eT[:, kt * D:(kt + 1) * D],
                         wbf["W2"][e][:, kt * 128:(kt + 1) * 128])
            return kb.dma_mark()

        w1_tok = [None, None]
        w1_tok[0] = load_w1(0, 0)
        w2_tok = load_w2(0)
        for e in range(NE):
            if e < NE - 1:
                w1_tok[(e + 1) % 2] = load_w1(e + 1, (e + 1) % 2)
            kb.wait_tok(nc.tensor, w1_tok[e % 2])
            w1v = w1eT[:, (e % 2) * 8 * DH:(e % 2 + 1) * 8 * DH]
            for c5 in range(TH // T5):
                sl5 = (c5 * T5, (c5 + 1) * T5)
                kb.bank_wait(nc.tensor, tag + "pb4")
                pe_c = kb.mmgrp(pb[4][:, :],
                                [(S["sel32"][0:8, e * 128:(e + 1) * 128],
                                  combT[:, sl5[0]:sl5[1]])])
                nc.vector.wait_ge(kb.PEs, pe_c)
                v_e = kb.v(nc.vector.tensor_copy(cbc[:, :], pb[4][:, :]))
                kb.bank_set(tag + "pb4", [(kb.Vs, v_e)])
                hrot = (0, 1, 6)
                for ht in range(16):
                    bi = hrot[ht % 3]
                    bk = tag + "pb%d" % bi
                    kb.bank_wait(nc.tensor, bk)
                    pe_h = kb.mmgrp(pb[bi][:, :],
                                    [(w1v[:, kt * DH + ht * 128:kt * DH + (ht + 1) * 128],
                                      dv(xln, kt, TH, sl5[0], sl5[1])) for kt in range(NDM)])
                    hs = hstage[:, (ht % 2) * T5:(ht % 2 + 1) * T5]
                    nc.scalar.wait_ge(kb.PEs, pe_h)
                    kb.bank_wait(nc.scalar, tag + "hstg%d" % (ht % 2))
                    s_e = kb.s(nc.scalar.activation(
                        hs, pb[bi][:, :], AF.Relu,
                        bias=S["b1c"][:, e * 16 + ht:e * 16 + ht + 1]))
                    kb.bank_set(bk, [(kb.Ss, s_e)])
                    nc.vector.wait_ge(kb.Ss, s_e)
                    v_e = kb.v(nc.vector.tensor_mul(dv(h1, ht, T5), hs, cbc[:, :]))
                    kb.bank_set(tag + "hstg%d" % (ht % 2), [(kb.Vs, v_e)])
                v_h = kb.Vn
                nc.tensor.wait_ge(kb.Vs, v_h)
                kb.wait_tok(nc.tensor, w2_tok)
                rot = (2, 3, 5)
                for m in range(NDM):
                    bi = rot[m % 3]
                    bk = tag + "pb%d" % bi
                    kb.bank_wait(nc.tensor, bk)
                    pe_m = kb.mmgrp(pb[bi][:, :],
                                    [(w2eT[:, kt * D + m * 128:kt * D + (m + 1) * 128],
                                      dv(h1, kt, T5)) for kt in range(16)])
                    nc.vector.wait_ge(kb.PEs, pe_m)
                    v_e = kb.v(nc.vector.tensor_add(dv(xacc, m, TH, sl5[0], sl5[1]),
                                                    pb[bi][:, :],
                                                    dv(xacc, m, TH, sl5[0], sl5[1])))
                    kb.bank_set(bk, [(kb.Vs, v_e)])
            kb.bank_set(tag + "w1s%d" % (e % 2), [(kb.PEs, kb.PEn)])
            kb.bank_set(tag + "w2s", [(kb.PEs, kb.PEn)])
            if e < NE - 1:
                w2_tok = load_w2(e + 1)
        for c5 in range(TH // T5):
            sl5 = (c5 * T5, (c5 + 1) * T5)
            for m in range(NDM):
                bk = tag + "pb%d" % (4 + m % 2)
                kb.bank_wait(nc.tensor, bk)
                pe_b = kb.mmgrp(pb[4 + (m % 2)][:, :], [(b2sb[:, m * 128:(m + 1) * 128],
                                                         combT[:, sl5[0]:sl5[1]])])
                nc.vector.wait_ge(kb.PEs, pe_b)
                v_e = kb.v(nc.vector.tensor_add(dv(xacc, m, TH, sl5[0], sl5[1]),
                                                pb[4 + (m % 2)][:, :],
                                                dv(xacc, m, TH, sl5[0], sl5[1])))
                kb.bank_set(bk, [(kb.Vs, v_e)])
        v_out = kb.Vn

        if out_dram is None:
            nc.sync.wait_ge(kb.Vs, v_out)
            t_sq = kb.sydma(xo_view[:, :, hoff:hoff + TH],
                            xacc[:, :].rearrange("p (t n) -> p t n", n=TH))
            kb.bank_set(tag + "qld", [(kb.PEs, kb.PEn), t_sq])
        else:
            of = out_dram.rearrange("b l d -> (b l) d")
            slot_tok = [None] * 3
            si = 0
            for ti in range(TH // 128):
                for m in range(NDM):
                    sl = ostage[:, (si % 3) * 128:(si % 3) * 128 + 128]
                    nc.tensor.wait_ge(kb.Vs, kb.Vn)
                    pe_t = kb.pe_inc(nc.tensor.transpose(
                        ptrf[:, 0:128], dv(xacc, m, TH, ti * 128, (ti + 1) * 128),
                        S["id_f"][:, :]))
                    nc.vector.wait_ge(kb.PEs, pe_t)
                    if slot_tok[si % 3] is not None:
                        kb.wait_tok(nc.vector, slot_tok[si % 3])
                    v_st = kb.v(nc.vector.tensor_copy(sl, ptrf[:, 0:128]))
                    nc.sync.wait_ge(kb.Vs, v_st)
                    slot_tok[si % 3] = kb.sydma(
                        of[hoff + ti * 128: hoff + (ti + 1) * 128, m * 128:(m + 1) * 128], sl)
                    si += 1
            kb.bank_set(tag + "qld", [(kb.PEs, kb.PEn)] + [t for t in slot_tok if t])


def _phase4(nc, kb, S, w, wbf, x2_hbm, x3_hbm):
    with ExitStack() as ctx:
        kb_ctx, kb.ctx = kb.ctx, ctx
        try:
            _phase4_body(nc, kb, S, wbf, x2_hbm, x3_hbm)
        finally:
            kb.barrier()
            kb.ctx = kb_ctx


def _phase4_body(nc, kb, S, wbf, x2_hbm, x3_hbm):
    CH = 256
    xact = kb.sb("p4_xact", [128, NDM * L], BF16)
    stb = kb.sb("p4_stb", [128, NDM * L], BF16)
    ybuf = kb.sb("p4_y", [128, NDM * L], F32)
    wTW = kb.sb("p4_wTW", [128, NDM * D], BF16)
    x2c = kb.sb("p4_x2c", [128, NDM * CH], F32)
    sq = kb.sb("p4_sq", [128, NDM * CH], F32)
    stat = kb.sb("p4_stat", [1, 3 * CH], F32)
    tmp = kb.sb("p4_tmp", [128, CH], F32)
    tmp2 = kb.sb("p4_tmp2", [128, CH], F32)
    Sst = kb.sb("p4_Sst", [128, 8 * 64], F32)
    Sbf = kb.sb("p4_Sbf", [128, 64], BF16)
    MT = kb.sb("p4_MT", [128, 2 * 512], BF16)
    kv = kb.sb("p4_kv", [128, 2 * 256], BF16)
    rhs0 = kb.sb("p4_rhs0", [128, 4 * 64], F32)
    Xb = kb.sb("p4_X", [128, 4 * 64], F32)
    Xbf = kb.sb("p4_Xbf", [128, 4 * 64], BF16)
    odyn = kb.sb("p4_odyn", [128, 4 * 64], BF16)
    wsc = kb.sb("p4_wsc", [128, 4 * 64], BF16)
    kq = kb.sb("p4_kq", [64, 2 * CH], F32)
    ksr = kb.sb("p4_ksr", [1, 2 * CH], F32)
    ccol = kb.sb("p4_ccol", [128, 4], F32)
    dst = kb.sb("p4_dst", [64, 2 * 64], F32)
    pbank = [kb.psum("p4_pb%d" % i, [128, 512], F32) for i in range(6)]
    pbf = kb.psum("p4_pbf", [128, 256], BF16)
    pbf2 = kb.psum("p4_pbf2", [128, 256], BF16)

    x2_view = x2_hbm.rearrange("(t p) T -> p t T", p=128)
    x3_view = x3_hbm.rearrange("(t p) T -> p t T", p=128)

    kb.wait_gq(nc.sync, "wb")
    for kt in range(NDM):
        kb.trdma(wTW[:, kt * D:(kt + 1) * D], wbf["Wb"][:, kt * 128:(kt + 1) * 128])
    t_w = kb.dma_mark()
    kb.wait_tok(nc.tensor, t_w)

    def hslice(buf, h, ci, lo, hi):
        dth = h // 2
        pb = (h % 2) * 64
        return buf[pb:pb + 64, dth * L + ci * CH + lo: dth * L + ci * CH + hi]

    for b in range(NB):
        boff = b * L
        kb.v(nc.vector.memset(Sst[:, :], 0.0))
        for ci in range(NCH):
            toff = boff + ci * CH
            nc.sync.wait_ge(kb.Ss, kb.Sn)
            nc.sync.wait_ge(kb.Vs, kb.Vn)
            kb.sydma(x2c[:, :].rearrange("p (t n) -> p t n", n=CH),
                     x2_view[:, :, toff:toff + CH])
            t_x = kb.dma_mark()
            kb.wait_tok(nc.vector, t_x)
            v_x = kb.v(nc.vector.tensor_copy(x2c[:, 0:1], x2c[:, 0:1]))
            pe_bc, mu_bc, rs_bc = _stats(nc, kb, S, x2c, sq, stat,
                                         pbank[4][0:1, 0:CH], pbank[5][0:1, 0:CH],
                                         pbank[0][:, 0:CH], pbank[1][:, 0:CH],
                                         NDM, CH, S["epsln"][:, :], False, v_x)
            nc.vector.wait_ge(kb.PEs, pe_bc)
            for t in range(NDM):
                nc.vector.wait_ge(kb.Ss, kb.Sn)
                kb.v(nc.vector.tensor_sub(tmp[:, :], dv(x2c, t, CH), mu_bc))
                v_t = kb.v(nc.vector.tensor_mul(tmp2[:, :], tmp[:, :], rs_bc))
                nc.scalar.wait_ge(kb.Vs, v_t)
                kb.s(nc.scalar.activation(xact[:, t * L + ci * CH: t * L + (ci + 1) * CH],
                                          tmp2[:, :], AF.Tanh))
        s_xact = kb.Sn
        nc.tensor.wait_ge(kb.Ss, s_xact)
        for ci in range(NCH):
            for m in range(NDM):
                pp = pbank[m % 2]
                bk = "p4b%d" % (m % 2)
                kb.bank_wait(nc.tensor, bk)
                pe_m = kb.mmgrp(pp[:, 0:CH],
                                [(wTW[:, kt * D + m * 128:kt * D + (m + 1) * 128],
                                  xact[:, kt * L + ci * CH: kt * L + (ci + 1) * CH])
                                 for kt in range(NDM)])
                nc.vector.wait_ge(kb.PEs, pe_m)
                kb.v(nc.vector.tensor_copy(stb[:, m * L + ci * CH: m * L + (ci + 1) * CH],
                                           pp[:, 0:CH]))
                v_e = kb.v(nc.vector.tensor_scalar_mul(
                    ybuf[:, m * L + ci * CH: m * L + (ci + 1) * CH], pp[:, 0:CH], 2.0))
                kb.bank_set(bk, [(kb.Vs, v_e)])
        def delta_gen(slot, h, ci):
            pbh = (h % 2) * 64
            sblk = (h // 2) * 64
            bA = pbank[3 * slot + 0]
            bT = pbank[3 * slot + 1]
            bM = pbank[3 * slot + 2]
            pqf = pbf if slot == 0 else pbf2
            MTs = MT[:, slot * 512:(slot + 1) * 512]
            kvs = kv[:, slot * 256:(slot + 1) * 256]
            kts = None
            vts = None
            r0s = rhs0[:, slot * 128:(slot + 1) * 128]
            Xs = Xb[:, slot * 128:(slot + 1) * 128]
            Xfs = Xbf[:, slot * 128:(slot + 1) * 128]
            ods = odyn[:, slot * 128:(slot + 1) * 128]
            wss = wsc[:, slot * 128:(slot + 1) * 128]
            kqs = kq[:, slot * CH:(slot + 1) * CH]
            ksrs = ksr[:, slot * CH:(slot + 1) * CH]
            ccs = ccol[:, slot * 2:slot * 2 + 2]
            dsts = dst[:, slot * 64:(slot + 1) * 64]
            bkA = "p4A%d" % slot
            bkT = "p4T%d" % slot
            bkM = "p4M%d" % slot
            bkQ = "p4Q%d" % slot

            kb.bank_wait(nc.tensor, bkA)
            for mt in range(2):
                kb.mmgrp(bA[:, mt * 256:(mt + 1) * 256],
                         [(hslice(xact, h, ci, mt * 128, (mt + 1) * 128),
                           hslice(xact, h, ci, 0, 256))])
            pe_A = kb.PEn
            nc.scalar.wait_ge(kb.PEs, kb.PEn)
            kb.s(nc.scalar.activation(kqs[:, :], hslice(xact, h, ci, 0, 256), AF.Square))
            s_kq = kb.Sn
            nc.vector.wait_ge(kb.PEs, pe_A)
            kb.v(nc.vector.tensor_mul(MTs[:, 0:256], bA[:, 0:256], S["Upre0"][:, :]))
            v_mt = kb.v(nc.vector.tensor_mul(MTs[:, 256:512], bA[:, 256:512], S["Upre1"][:, :]))
            kb.bank_set(bkA, [(kb.Vs, v_mt)])
            yield

            nc.tensor.wait_ge(kb.Ss, s_kq)
            kb.bank_wait(nc.tensor, bkT)
            kb.mmgrp(bT[0:1, 128:384], [(S["ones_col_f"][0:64, :], kqs[:, :])])
            pe_kq = kb.PEn
            nc.vector.wait_ge(kb.PEs, pe_kq)
            v_ksr = kb.v(nc.vector.tensor_copy(ksrs[:, :], bT[0:1, 128:384]))
            nc.tensor.wait_ge(kb.Vs, v_ksr)
            for st in range(2):
                kb.pe_inc(nc.tensor.transpose(bT[:, 384 + st:384 + st + 1],
                                              ksrs[0:1, st * 128:(st + 1) * 128],
                                              S["id_f"][0:1, 0:1]))
            pe_cc = kb.PEn
            nc.vector.wait_ge(kb.PEs, pe_cc)
            for st in range(2):
                kb.v(nc.vector.tensor_scalar(ccs[:, st:st + 1], bT[:, 384 + st:384 + st + 1],
                                             S["beta_c"][:, :], 1.0, ALU.mult, ALU.subtract))
            v_cc = kb.v(nc.vector.tensor_copy(Sbf[pbh:pbh + 64, :],
                                              Sst[pbh:pbh + 64, sblk:sblk + 64]))
            kb.bank_set(bkT, [(kb.Vs, v_cc)])
            idb = S["id_bf"][pbh:pbh + 64, pbh:pbh + 64]
            for st in range(2):
                kb.bank_wait(nc.tensor, bkQ)
                kb.pe_inc(nc.tensor.transpose(
                    pqf[:, 0:64], hslice(xact, h, ci, st * 128, (st + 1) * 128), idb))
                kb.pe_inc(nc.tensor.transpose(
                    pqf[:, 64:128], hslice(stb, h, ci, st * 128, (st + 1) * 128), idb))
                nc.vector.wait_ge(kb.PEs, kb.PEn)
                kb.v(nc.vector.tensor_copy(kvs[:, st * 128:(st + 1) * 128], pqf[:, 0:128]))
                kb.bank_set(bkQ, [(kb.Vs, kb.Vn)])
            yield

            nc.tensor.wait_ge(kb.Vs, v_cc)
            kb.bank_wait(nc.tensor, bkT)
            for mt in range(2):
                kb.mmgrp(bT[:, mt * 64:(mt + 1) * 64],
                         [(hslice(xact, h, ci, mt * 128, (mt + 1) * 128),
                           Sbf[pbh:pbh + 64, :])])
            pe_T1 = kb.PEn
            nc.vector.wait_ge(kb.PEs, pe_T1)
            for st in range(2):
                kb.v(nc.vector.scalar_tensor_tensor(
                    r0s[:, st * 64:(st + 1) * 64], bT[:, st * 64:(st + 1) * 64],
                    S["nlamj1"][:, st:st + 1], kvs[:, st * 128 + 64:st * 128 + 128],
                    ALU.mult, ALU.add))
            v_X = kb.v(nc.vector.tensor_copy(Xfs[:, :], r0s[:, :]))
            kb.bank_set(bkT, [(kb.Vs, v_X)])
            yield

            for it in range(3):
                nc.tensor.wait_ge(kb.Vs, v_X)
                kb.bank_wait(nc.tensor, bkM)
                for tt in range(2):
                    kb.mmgrp(bM[:, tt * 64:(tt + 1) * 64],
                             [(MTs[:, 0 + tt * 128:0 + (tt + 1) * 128], Xfs[:, 0:64]),
                              (MTs[:, 256 + tt * 128:256 + (tt + 1) * 128], Xfs[:, 64:128])])
                pe_mx = kb.PEn
                nc.vector.wait_ge(kb.PEs, pe_mx)
                for tt in range(2):
                    kb.v(nc.vector.tensor_sub(Xs[:, tt * 64:(tt + 1) * 64],
                                              r0s[:, tt * 64:(tt + 1) * 64],
                                              bM[:, tt * 64:(tt + 1) * 64]))
                if it < 2:
                    v_X = kb.v(nc.vector.tensor_copy(Xfs[:, :], Xs[:, :]))
                kb.bank_set(bkM, [(kb.Vs, kb.Vn)])
                yield

            for st in range(2):
                kb.v(nc.vector.tensor_scalar(ods[:, st * 64:(st + 1) * 64],
                                             Xs[:, st * 64:(st + 1) * 64],
                                             ccs[:, st:st + 1], None, ALU.mult))
                kb.v(nc.vector.tensor_scalar(wss[:, st * 64:(st + 1) * 64],
                                             Xs[:, st * 64:(st + 1) * 64],
                                             S["brev"][:, st:st + 1], None, ALU.mult))
            v_ow = kb.Vn
            nc.tensor.wait_ge(kb.Vs, v_ow)
            kb.bank_wait(nc.tensor, bkQ)
            for st in range(2):
                kb.pe_inc(nc.tensor.transpose(pqf[0:64, st * 128:(st + 1) * 128],
                                              ods[:, st * 64:(st + 1) * 64],
                                              S["id_bf"][:, :]))
            nc.vector.wait_ge(kb.PEs, kb.PEn)
            kb.v(nc.vector.tensor_add(
                hslice(ybuf, h, ci, 0, 256), hslice(ybuf, h, ci, 0, 256),
                pqf[0:64, 0:256]))
            kb.bank_set(bkQ, [(kb.Vs, kb.Vn)])
            yield

            kb.bank_wait(nc.tensor, bkT)
            kb.mmgrp(bT[0:64, 128:192],
                     [(wss[:, 0:64], kvs[:, 0:64]), (wss[:, 64:128], kvs[:, 128:192])])
            pe_dst = kb.PEn
            nc.vector.wait_ge(kb.PEs, pe_dst)
            v_dst = kb.v(nc.vector.tensor_copy(dsts[:, :], bT[0:64, 128:192]))
            nc.tensor.wait_ge(kb.Vs, v_dst)
            kb.pe_inc(nc.tensor.transpose(bT[0:64, 192:256], dsts[:, :],
                                          S["id_f"][0:64, 0:64]))
            nc.vector.wait_ge(kb.PEs, kb.PEn)
            v_s = kb.v(nc.vector.scalar_tensor_tensor(
                Sst[pbh:pbh + 64, sblk:sblk + 64], Sst[pbh:pbh + 64, sblk:sblk + 64],
                S["lamQ_c"][pbh:pbh + 64, :], bT[0:64, 192:256], ALU.mult, ALU.add))
            kb.bank_set(bkT, [(kb.Vs, v_s)])

        _sent = object()
        nc.tensor.wait_ge(kb.Vs, kb.Vn)
        for pair in range(8):
            for ci in range(NQ):
                g0 = delta_gen(0, pair * 2, ci)
                g1 = delta_gen(1, pair * 2 + 1, ci)
                alive = [g0, g1]
                while alive:
                    for g in list(alive):
                        if next(g, _sent) is _sent:
                            alive.remove(g)
        prev_store = None
        for ci in range(NCH):
            toff = boff + ci * CH
            nc.sync.wait_ge(kb.Vs, kb.Vn)
            nc.sync.wait_ge(kb.Ss, kb.Sn)
            if prev_store is not None:
                kb.wait_tok(nc.sync, prev_store)
            kb.sydma(x2c[:, :].rearrange("p (t n) -> p t n", n=CH),
                     x2_view[:, :, toff:toff + CH])
            t_x = kb.dma_mark()
            kb.wait_tok(nc.vector, t_x)
            ytiles = [ybuf[:, t * L + ci * CH: t * L + (ci + 1) * CH] for t in range(NDM)]
            pe_bc, mu_bc, rs_bc = _stats(nc, kb, S, None, sq, stat,
                                         pbank[4][0:1, 0:CH], pbank[5][0:1, 0:CH],
                                         pbank[0][:, 0:CH], pbank[1][:, 0:CH],
                                         NDM, CH, S["epsln"][:, :], False, kb.Vn,
                                         src_tiles=ytiles)
            nc.vector.wait_ge(kb.PEs, pe_bc)
            for t in range(NDM):
                kb.v(nc.vector.tensor_sub(tmp[:, :], ytiles[t], mu_bc))
                kb.v(nc.vector.tensor_mul(tmp2[:, :], tmp[:, :], rs_bc))
                kb.v(nc.vector.scalar_tensor_tensor(
                    tmp[:, :], tmp2[:, :], S["lng"][:, t:t + 1], dv(x2c, t, CH),
                    ALU.mult, ALU.add))
                kb.v(nc.vector.tensor_scalar(dv(x2c, t, CH), tmp[:, :],
                                             S["lnb"][:, t:t + 1], None, ALU.add))
            v_x3 = kb.Vn
            nc.sync.wait_ge(kb.Vs, v_x3)
            prev_store = kb.sydma(x3_view[:, :, toff:toff + CH],
                                  x2c[:, :].rearrange("p (t n) -> p t n", n=CH))
        kb.barrier()



PHASES = (1, 2, 3, 4, 5)


def build_program(debug=False, phases=PHASES):
    nc = bass.Bass("TRN2", target_bir_lowering=False)
    xin = nc.dram_tensor("xin", [NB, L, D], F32, kind="ExternalInput")
    w = {}
    for name, shape in [
        ("in_proj", [2 * DI, D]), ("conv_w", [DI, 4]), ("conv_b", [DI]),
        ("x_proj", [DTR + 2 * NS, DI]), ("dt_proj", [DI, DTR]), ("dt_b", [DI]),
        ("A_log", [DI, NS]), ("Dv", [DI]), ("out_proj", [D, DI]),
        ("mlp_w1", [DFF, D]), ("mlp_b1", [DFF]), ("mlp_w2", [D, DFF]),
        ("mlp_b2", [D]), ("norm1_w", [D]),
        ("Wb", [D, D]), ("ln_g", [D]), ("ln_b", [D]), ("eta", [1, 1]), ("lam", [1, 1]),
        ("W1", [NE, DH, D]), ("b1", [NE, DH]), ("W2", [NE, D, DH]),
        ("b2", [NE, D]), ("Wg", [NE, D]), ("bg", [NE, 1]),
    ]:
        w[name] = nc.dram_tensor(name, shape, F32, kind="ExternalInput")
    out = nc.dram_tensor("out", [NB, L, D], F32, kind="ExternalOutput")

    sk = "ExternalOutput" if debug else "Internal"
    xd_hbm = nc.dram_tensor("xd_hbm", [D, TOK], F32, kind=sk)
    zm_hbm = nc.dram_tensor("zm_hbm", [D, TOK], BF16, kind=sk)
    x1_hbm = nc.dram_tensor("x1_hbm", [D, TOK], F32, kind=sk)
    x2_hbm = nc.dram_tensor("x2_hbm", [D, TOK], F32, kind=sk)
    x3_hbm = nc.dram_tensor("x3_hbm", [D, TOK], F32, kind=sk)

    wbf = {}
    for name in ["in_proj", "x_proj", "dt_proj", "out_proj",
                 "mlp_w1", "mlp_w2", "Wb", "W1", "W2", "Wg", "b2"]:
        wbf[name] = nc.dram_tensor(name + "_bf", list(w[name].shape), BF16, kind="Internal")

    with ExitStack() as ctx:
        kb = KB(nc, ctx)
        S = _build_setup(nc, kb, w, wbf)
        kb.barrier()
        if 1 in phases:
            _phase1(nc, kb, S, xin, w, wbf, xd_hbm, zm_hbm)
        if 2 in phases:
            _phase2(nc, kb, S, w, wbf, xd_hbm, zm_hbm, x1_hbm)
            kb.barrier()
        if 3 in phases:
            _phase_ffn(nc, kb, S, w, wbf, x1_hbm, x2_hbm, None, tag="f1")
            kb.barrier()
        if 4 in phases:
            _phase4(nc, kb, S, w, wbf, x2_hbm, x3_hbm)
            kb.barrier()
        if 5 in phases:
            _phase_ffn(nc, kb, S, w, wbf, x3_hbm, None, out, tag="f2")
            kb.barrier()
        for grp in kb.gq_sems:
            kb.wait_gq(nc.gpsimd, grp)
    return nc


def make_in_map(inputs, core):
    b0 = core * NB
    return {
        "xin": np.ascontiguousarray(inputs["L0_attn_a_Z"][b0:b0 + NB]),
        "in_proj": inputs["L0_attn_a_in_proj_w"],
        "conv_w": inputs["L0_attn_a_conv_w"],
        "conv_b": inputs["L0_attn_a_conv_b"],
        "x_proj": inputs["L0_attn_a_x_proj_w"],
        "dt_proj": inputs["L0_attn_a_dt_proj_w"],
        "dt_b": inputs["L0_attn_a_dt_proj_b"],
        "A_log": inputs["L0_attn_a_A_log"],
        "Dv": inputs["L0_attn_a_D"],
        "out_proj": inputs["L0_attn_a_out_proj_w"],
        "mlp_w1": inputs["L0_attn_a_mlp_w1"],
        "mlp_b1": inputs["L0_attn_a_mlp_b1"],
        "mlp_w2": inputs["L0_attn_a_mlp_w2"],
        "mlp_b2": inputs["L0_attn_a_mlp_b2"],
        "norm1_w": inputs["L0_attn_a_norm1_w"],
        "Wb": inputs["L0_attn_b_W"],
        "ln_g": inputs["L0_attn_b_ln_gamma"],
        "ln_b": inputs["L0_attn_b_ln_beta"],
        "eta": np.reshape(inputs["L0_attn_b_eta"], (1, 1)),
        "lam": np.reshape(inputs["L0_attn_b_lam_logit"], (1, 1)),
        "W1": inputs["L0_ffn_W1"],
        "b1": inputs["L0_ffn_b1"],
        "W2": inputs["L0_ffn_W2"],
        "b2": inputs["L0_ffn_b2"],
        "Wg": inputs["L0_ffn_Wg"],
        "bg": np.reshape(inputs["L0_ffn_bg"], (NE, 1)),
    }


def kernel(**inputs):
    inputs = {k: np.asarray(v, dtype=np.float32) for k, v in inputs.items()}
    nc = build_program(debug=False, phases=PHASES)
    in_maps = [make_in_map(inputs, c) for c in range(8)]
    res = run_bass_kernel_spmd(nc, in_maps, core_ids=list(range(8)))
    return np.concatenate([np.asarray(r["out"]) for r in res.results], axis=0)


# version 19
# speedup vs baseline: 1.0315x; 1.0315x over previous
import numpy as np
from contextlib import ExitStack

import concourse.bass as bass
import concourse.mybir as mybir
from concourse.bass_utils import run_bass_kernel_spmd

F32 = mybir.dt.float32
BF16 = mybir.dt.bfloat16
I32 = mybir.dt.int32
AF = mybir.ActivationFunctionType
ALU = mybir.AluOpType
AX = mybir.AxisListType

NB = 2
L = 2048
D = 1024
DI = 2048
NS = 16
DTR = 64
DFF = 4096
NE = 8
DH = 2048
TOK = NB * L
LC = 256
NCH = L // LC
QD = 256
NQ = L // QD
NDT = DI // 128
NDM = D // 128
LN_EPS = 1e-5
RMS_EPS = 1e-6


class KB:
    def __init__(self, nc, ctx):
        self.nc = nc
        self.ctx = ctx
        sem = lambda name: ctx.enter_context(nc.semaphore(name))
        self.Vs, self.Vn = sem("V"), 0
        self.Ss, self.Sn = sem("S"), 0
        self.Gs, self.Gn = sem("G"), 0
        self.PEs, self.PEn = sem("PE"), 0
        self.dma_sems = [sem("dma%d" % i) for i in range(4)]
        self.dma_cnt = [0, 0, 0, 0]
        self.dma_rr = 0
        self.gq_sems = {}
        self.banks = {}

    def bank_wait(self, eng, key):
        for sem, val in self.banks.get(key, []):
            eng.wait_ge(sem, val)

    def bank_set(self, key, toks):
        self.banks[key] = toks

    def sb(self, name, shape, dt=F32):
        return self.ctx.enter_context(self.nc.sbuf_tensor(name, shape, dt))

    def psum(self, name, shape, dt=F32):
        return self.ctx.enter_context(self.nc.psum_tensor(name, shape, dt))

    def v(self, instr):
        if self.Vn > 0:
            instr.wait_op(self.Vs, self.Vn, "sem-ge")
        instr.then_inc(self.Vs, 1)
        self.Vn += 1
        return self.Vn

    def s(self, instr):
        if self.Sn > 0:
            instr.wait_op(self.Ss, self.Sn, "sem-ge")
        instr.then_inc(self.Ss, 1)
        self.Sn += 1
        return self.Sn

    def g(self, instr):
        if self.Gn > 0:
            instr.wait_op(self.Gs, self.Gn, "sem-ge")
        instr.then_inc(self.Gs, 1)
        self.Gn += 1
        return self.Gn

    def pe_inc(self, instr):
        instr.then_inc(self.PEs, 1)
        self.PEn += 1
        return self.PEn

    def mmgrp(self, out_ap, ops):
        n = len(ops)
        last = None
        for i, (a, b) in enumerate(ops):
            last = self.nc.tensor.matmul(out_ap, a, b, start=(i == 0), stop=(i == n - 1))
        return self.pe_inc(last)

    def _dma(self, out, in_, transpose):
        k = self.dma_rr
        self.dma_rr = (self.dma_rr + 1) % 4
        s = self.dma_sems[k]
        c = self.dma_cnt[k]
        f = self.nc.sync.dma_start_transpose if transpose else self.nc.sync.dma_start
        instr = f(out, in_)
        if c > 0:
            instr.wait_op(s, 16 * c, "sem-ge")
        instr.then_inc(s, 16)
        self.dma_cnt[k] = c + 1
        return (s, 16 * (c + 1))

    def sydma(self, out, in_):
        return self._dma(out, in_, False)

    def trdma(self, out, in_):
        return self._dma(out, in_, True)

    def dma_mark(self):
        return [(s, 16 * c) for s, c in zip(self.dma_sems, self.dma_cnt) if c > 0]

    def wait_tok(self, eng, tok):
        if isinstance(tok, list):
            for s, v in tok:
                eng.wait_ge(s, v)
        else:
            eng.wait_ge(tok[0], tok[1])

    def gqdma(self, group, out, in_):
        if group not in self.gq_sems:
            self.gq_sems[group] = [self.ctx.enter_context(self.nc.semaphore("gq_" + group)), 0]
        s, c = self.gq_sems[group]
        instr = self.nc.gpsimd.dma_start(out, in_)
        if c > 0:
            instr.wait_op(s, c, "sem-ge")
        instr.then_inc(s, 16)
        self.gq_sems[group][1] = c + 16
        return self.gq_sems[group][1]

    def wait_gq(self, eng, group):
        s, n = self.gq_sems[group]
        if n > 0:
            eng.wait_ge(s, n)

    def barrier(self):
        nc = self.nc
        targets = [(self.Vs, self.Vn), (self.Ss, self.Sn), (self.Gs, self.Gn),
                   (self.PEs, self.PEn)]
        targets += [(s, 16 * c) for s, c in zip(self.dma_sems, self.dma_cnt)]
        for eng in (nc.vector, nc.scalar, nc.gpsimd, nc.tensor, nc.sync):
            for sem, val in targets:
                if val > 0:
                    eng.wait_ge(sem, val)


def dv(t, tile, n, lo=None, hi=None):
    if lo is None:
        return t[:, tile * n:(tile + 1) * n]
    return t[:, tile * n + lo: tile * n + hi]



def _build_setup(nc, kb, w, wbf):
    S = {}
    for group, names in [("a", ["in_proj", "x_proj", "dt_proj", "out_proj"]),
                         ("mlp", ["mlp_w1", "mlp_w2"]),
                         ("ffn", ["W1", "W2", "Wg", "b2"]),
                         ("wb", ["Wb"])]:
        for n in names:
            kb.gqdma(group, wbf[n][...], w[n][...])

    def vec_cols(name, dram, n):
        t = kb.sb(name, [128, n // 128], F32)
        view = dram.rearrange("(c p) -> c p", p=128)
        for j in range(n // 128):
            kb.sydma(t[:, j:j + 1], view[j:j + 1, :])
        return t

    S["convb"] = vec_cols("c_convb", w["conv_b"], DI)
    S["dtb"] = vec_cols("c_dtb", w["dt_b"], DI)
    S["Dvec"] = vec_cols("c_Dvec", w["Dv"], DI)
    S["n1w"] = vec_cols("c_n1w", w["norm1_w"], D)
    S["lng"] = vec_cols("c_lng", w["ln_g"], D)
    S["lnb"] = vec_cols("c_lnb", w["ln_b"], D)
    S["mlpb1"] = vec_cols("c_mlpb1", w["mlp_b1"], DFF)
    S["mlpb2"] = vec_cols("c_mlpb2", w["mlp_b2"], D)
    S["b1c"] = kb.sb("c_b1c", [128, NE * 16], F32)
    b1v = w["b1"].rearrange("e (c p) -> (e c) p", p=128)
    for j in range(NE * 16):
        kb.sydma(S["b1c"][:, j:j + 1], b1v[j:j + 1, :])

    S["convw"] = kb.sb("c_convw", [128, NDT * 4], F32)
    cw = w["conv_w"].rearrange("(t p) k -> t p k", p=128)
    for t in range(NDT):
        kb.sydma(S["convw"][:, t * 4:(t + 1) * 4], cw[t])

    S["Aneg"] = kb.sb("c_Aneg", [128, NDT * NS], F32)
    al = w["A_log"].rearrange("(t p) n -> p t n", p=128)
    t_alog = kb.sydma(S["Aneg"][:, :].rearrange("p (t n) -> p t n", n=NS), al)

    S["bg8"] = kb.sb("c_bg8", [8, 1], F32)
    kb.sydma(S["bg8"][:, :], w["bg"][:, :])
    S["eta1"] = kb.sb("c_eta1", [1, 1], F32)
    S["laml"] = kb.sb("c_laml", [1, 1], F32)
    t_eta = kb.sydma(S["eta1"][:, :], w["eta"][:, :])
    t_scal = kb.sydma(S["laml"][:, :], w["lam"][:, :])

    S["id_f"] = kb.sb("c_idf", [128, 128], F32)
    S["id_bf"] = kb.sb("c_idbf", [128, 128], BF16)
    S["ones_row_f"] = kb.sb("c_or_f", [1, 128], F32)
    S["ones_col_f"] = kb.sb("c_oc_f", [128, 1], F32)
    S["ones_col_bf"] = kb.sb("c_oc_bf", [128, 1], BF16)
    S["sel32"] = kb.sb("c_sel32", [32, 32 * 128], BF16)
    S["Upre0"] = kb.sb("c_Upre0", [128, 256], F32)
    S["Upre1"] = kb.sb("c_Upre1", [128, 256], F32)
    S["llog_c"] = kb.sb("c_llog_c", [128, 1], F32)
    S["beta_c"] = kb.sb("c_beta_c", [128, 1], F32)
    S["lamQ_c"] = kb.sb("c_lamQ_c", [128, 1], F32)
    S["nlamj1"] = kb.sb("c_nlamj1", [128, 2], F32)
    S["brev"] = kb.sb("c_brev", [128, 2], F32)
    S["epsln"] = kb.sb("c_epsln", [1, 1], F32)
    S["epsrms"] = kb.sb("c_epsrms", [1, 1], F32)

    tmp_ctx = ExitStack()
    _outer_ctx, kb.ctx = kb.ctx, tmp_ctx
    ii = kb.sb("c_ii", [128, 256], I32)
    iif = kb.sb("c_iif", [128, 256], F32)
    kb.g(nc.gpsimd.iota(ii[:, 0:128], [[1, 128]], base=0, channel_multiplier=-1))
    nc.vector.wait_ge(kb.Gs, kb.Gn)
    kb.v(nc.vector.tensor_scalar(S["id_f"][:, :], ii[:, 0:128], 0.0, None, ALU.is_equal))
    kb.v(nc.vector.tensor_copy(S["id_bf"][:, :], S["id_f"][:, :]))
    kb.v(nc.vector.memset(S["ones_row_f"][:, :], 1.0))
    kb.v(nc.vector.memset(S["ones_col_f"][:, :], 1.0))
    kb.v(nc.vector.memset(S["ones_col_bf"][:, :], 1.0))
    kb.v(nc.vector.memset(S["epsln"][:, :], LN_EPS))
    kb.v(nc.vector.memset(S["epsrms"][:, :], RMS_EPS))

    sel_i = kb.sb("c_sel_i", [32, 32 * 128], I32)
    kb.g(nc.gpsimd.iota(sel_i[:, :], [[-1, 32], [0, 128]], base=0, channel_multiplier=1))
    nc.vector.wait_ge(kb.Gs, kb.Gn)
    kb.v(nc.vector.tensor_scalar(S["sel32"][:, :], sel_i[:, :], 0.0, None, ALU.is_equal))

    kb.wait_tok(nc.scalar, t_alog)
    kb.s(nc.scalar.activation(S["Aneg"][:, :], S["Aneg"][:, :], AF.Exp))
    nc.vector.wait_ge(kb.Ss, kb.Sn)
    kb.v(nc.vector.tensor_scalar_mul(S["Aneg"][:, :], S["Aneg"][:, :], -1.0))

    lam1 = kb.sb("c_lam1", [1, 1], F32)
    llog1 = kb.sb("c_llog1", [1, 1], F32)
    beta1 = kb.sb("c_beta1", [1, 1], F32)
    lamQ1 = kb.sb("c_lamQ1", [1, 1], F32)
    kb.wait_tok(nc.scalar, t_scal)
    kb.wait_tok(nc.vector, t_eta)
    kb.s(nc.scalar.activation(lam1[:, :], S["laml"][:, :], AF.Sigmoid))
    kb.s(nc.scalar.activation(llog1[:, :], lam1[:, :], AF.Ln))
    nc.vector.wait_ge(kb.Ss, kb.Sn)
    kb.v(nc.vector.tensor_scalar(beta1[:, :], lam1[:, :], -1.0, 1.0, ALU.mult, ALU.add))
    kb.v(nc.vector.tensor_mul(beta1[:, :], beta1[:, :], S["eta1"][:, :]))
    v_beta = kb.v(nc.vector.tensor_scalar_mul(beta1[:, :], beta1[:, :], 1.0 / 64.0))
    nc.scalar.wait_ge(kb.Vs, v_beta)
    s_lamq = kb.s(nc.scalar.activation(lamQ1[:, :], llog1[:, :], AF.Exp, scale=float(QD)))

    pscal = kb.psum("c_pscal", [128, 4], F32)
    nc.tensor.wait_ge(kb.Ss, s_lamq)
    nc.tensor.wait_ge(kb.Vs, v_beta)
    nc.tensor.matmul(pscal[:, 0:1], S["ones_row_f"][:, :], llog1[:, :], start=True, stop=True)
    nc.tensor.matmul(pscal[:, 1:2], S["ones_row_f"][:, :], beta1[:, :], start=True, stop=True)
    pe_sc = kb.pe_inc(nc.tensor.matmul(pscal[:, 2:3], S["ones_row_f"][:, :], lamQ1[:, :], start=True, stop=True))
    nc.vector.wait_ge(kb.PEs, pe_sc)
    kb.v(nc.vector.tensor_copy(S["llog_c"][:, :], pscal[:, 0:1]))
    kb.v(nc.vector.tensor_copy(S["beta_c"][:, :], pscal[:, 1:2]))
    kb.v(nc.vector.tensor_copy(S["lamQ_c"][:, :], pscal[:, 2:3]))

    di0 = kb.sb("c_di0", [128, 256], I32)
    di1 = kb.sb("c_di1", [128, 256], I32)
    kb.g(nc.gpsimd.iota(di0[:, :], [[1, 256]], base=0, channel_multiplier=-1))
    kb.g(nc.gpsimd.iota(di1[:, :], [[1, 256]], base=-128, channel_multiplier=-1))
    nc.vector.wait_ge(kb.Gs, kb.Gn)
    for dsrc, udst in [(di0, S["Upre0"]), (di1, S["Upre1"])]:
        v_i = kb.v(nc.vector.tensor_copy(iif[:, :], dsrc[:, :]))
        nc.scalar.wait_ge(kb.Vs, v_i)
        s_e = kb.s(nc.scalar.activation(udst[:, :], iif[:, :], AF.Exp, scale=S["llog_c"][:, :]))
        nc.vector.wait_ge(kb.Ss, s_e)
        kb.v(nc.vector.tensor_scalar(iif[:, :], dsrc[:, :], 0.0, None, ALU.is_gt))
        kb.v(nc.vector.tensor_mul(udst[:, :], udst[:, :], iif[:, :]))
        kb.v(nc.vector.tensor_scalar(udst[:, :], udst[:, :], S["beta_c"][:, :], None, ALU.mult))

    ci = kb.sb("c_ci", [128, 1], I32)
    cf = kb.sb("c_cf", [128, 1], F32)
    for st in range(2):
        nc.gpsimd.wait_ge(kb.Vs, kb.Vn)
        kb.g(nc.gpsimd.iota(ci[:, :], [[0, 1]], base=1 + 128 * st, channel_multiplier=1))
        nc.vector.wait_ge(kb.Gs, kb.Gn)
        v_c = kb.v(nc.vector.tensor_copy(cf[:, :], ci[:, :]))
        nc.scalar.wait_ge(kb.Vs, v_c)
        s_c = kb.s(nc.scalar.activation(S["nlamj1"][:, st:st + 1], cf[:, :], AF.Exp, scale=S["llog_c"][:, :]))
        nc.vector.wait_ge(kb.Ss, s_c)
        kb.v(nc.vector.tensor_scalar_mul(S["nlamj1"][:, st:st + 1], S["nlamj1"][:, st:st + 1], -1.0))
        nc.gpsimd.wait_ge(kb.Vs, kb.Vn)
        kb.g(nc.gpsimd.iota(ci[:, :], [[0, 1]], base=QD - 1 - 128 * st, channel_multiplier=-1))
        nc.vector.wait_ge(kb.Gs, kb.Gn)
        v_c = kb.v(nc.vector.tensor_copy(cf[:, :], ci[:, :]))
        nc.scalar.wait_ge(kb.Vs, v_c)
        s_c = kb.s(nc.scalar.activation(S["brev"][:, st:st + 1], cf[:, :], AF.Exp, scale=S["llog_c"][:, :]))
        nc.vector.wait_ge(kb.Ss, s_c)
        kb.v(nc.vector.tensor_scalar(S["brev"][:, st:st + 1], S["brev"][:, st:st + 1], S["beta_c"][:, :], None, ALU.mult))
    kb.ctx = _outer_ctx
    kb.barrier()
    tmp_ctx.close()
    return S



def _stats(nc, kb, S, src, sq, stat, ps_sum, ps_ssq, ps_bmu, ps_brs,
           ntiles, n, eps_ap, rms, v_src, src_tiles=None):
    inv = 1.0 / (ntiles * 128)
    tiles = src_tiles if src_tiles is not None else [dv(src, t, n) for t in range(ntiles)]
    nc.tensor.wait_ge(kb.Vs, kb.Vn)
    nc.scalar.wait_ge(kb.Vs, v_src)
    for t in range(ntiles):
        kb.s(nc.scalar.activation(dv(sq, t, n), tiles[t], AF.Square))
    s_sq = kb.Sn
    nc.tensor.wait_ge(kb.Ss, s_sq)
    if not rms:
        nc.tensor.wait_ge(kb.Vs, v_src)
        kb.mmgrp(ps_sum, [(S["ones_col_f"][:, :], tiles[t]) for t in range(ntiles)])
    ones_sq = S["ones_col_bf"] if sq.dtype == BF16 else S["ones_col_f"]
    pe_ssq = kb.mmgrp(ps_ssq, [(ones_sq[:, :], dv(sq, t, n)) for t in range(ntiles)])
    mu = stat[0:1, 0:n]
    var = stat[0:1, n:2 * n]
    m2 = stat[0:1, 2 * n:3 * n]
    nc.vector.wait_ge(kb.PEs, pe_ssq)
    if rms:
        v_var = kb.v(nc.vector.tensor_scalar_mul(var, ps_ssq, inv))
    else:
        kb.v(nc.vector.tensor_scalar_mul(mu, ps_sum, inv))
        kb.v(nc.vector.tensor_mul(m2, mu, mu))
        v_var = kb.v(nc.vector.scalar_tensor_tensor(var, ps_ssq, inv, m2,
                                                    ALU.mult, ALU.subtract))
    nc.scalar.wait_ge(kb.Vs, v_var)
    kb.s(nc.scalar.activation(m2, var, AF.Ln, bias=eps_ap))
    s_r = kb.s(nc.scalar.activation(var, m2, AF.Exp, scale=-0.5))
    nc.tensor.wait_ge(kb.Ss, s_r)
    nc.tensor.wait_ge(kb.Vs, kb.Vn)
    mu_bc = None
    if not rms:
        kb.mmgrp(ps_bmu, [(S["ones_row_f"][:, :], mu)])
        mu_bc = ps_bmu
    pe_bc = kb.mmgrp(ps_brs, [(S["ones_row_f"][:, :], var)])
    return pe_bc, mu_bc, ps_brs



def _phase1(nc, kb, S, xin, w, wbf, xd_hbm, zm_hbm):
    with ExitStack() as ctx:
        kb_ctx, kb.ctx = kb.ctx, ctx
        try:
            _phase1_body(nc, kb, S, xin, w, wbf, xd_hbm, zm_hbm)
        finally:
            kb.barrier()
            kb.ctx = kb_ctx


def _phase1_body(nc, kb, S, xin, w, wbf, xd_hbm, zm_hbm):
    wTin = kb.sb("p1_wTin", [128, 8 * (2 * DI)], BF16)
    wTxp = kb.sb("p1_wTxp", [128, NDT * 96], BF16)
    wTdt = kb.sb("p1_wTdt", [64, NDT * 128], BF16)

    kb.wait_gq(nc.sync, "a")
    for kt in range(8):
        kb.trdma(wTin[:, kt * 4096:(kt + 1) * 4096], wbf["in_proj"][:, kt * 128:(kt + 1) * 128])
    for kt in range(NDT):
        kb.trdma(dv(wTxp, kt, 96), wbf["x_proj"][:, kt * 128:(kt + 1) * 128])
    dtp_ctx = ExitStack()
    _octx, kb.ctx = kb.ctx, dtp_ctx
    dtp_n = kb.sb("p1_dtp_n", [128, NDT * DTR], BF16)
    kb.ctx = _octx
    t_dtp = kb.sydma(dtp_n[:, :].rearrange("p (t r) -> p t r", r=DTR),
                     wbf["dt_proj"].rearrange("(t p) r -> p t r", p=128))
    t_wts = kb.dma_mark()

    p_tr = kb.psum("p1_ptr", [64, 256], BF16)
    kb.wait_tok(nc.tensor, t_dtp)
    for kt in range(NDT):
        half = kt % 2
        nc.tensor.wait_ge(kb.Vs, kb.Vn)
        pe_t = kb.pe_inc(nc.tensor.transpose(p_tr[:, half * 128:(half + 1) * 128],
                                             dv(dtp_n, kt, DTR), S["id_bf"][:, :]))
        nc.vector.wait_ge(kb.PEs, pe_t)
        kb.v(nc.vector.tensor_copy(dv(wTdt, kt, 128), p_tr[:, half * 128:(half + 1) * 128]))
    kb.barrier()
    dtp_ctx.close()

    xd = kb.sb("p1_xd", [128, NDM * LC], F32)
    sq = kb.sb("p1_sq", [128, NDM * LC], BF16)
    xstage = kb.sb("p1_xstage", [128, 1024], F32)
    zln = kb.sb("p1_zln", [128, NDM * LC], BF16)
    tmp8 = kb.sb("p1_tmp8", [128, LC], F32)
    cvt = kb.sb("p1_cvt", [128, 2 * LC], F32)
    stat = kb.sb("p1_stat", [1, 3 * LC], F32)
    xcin = kb.sb("p1_xcin", [128, NDT * (LC + 4)], BF16)
    z_sil = kb.sb("p1_zsil", [128, NDT * LC], BF16)
    xc_bf = kb.sb("p1_xcbf", [128, NDT * LC], BF16)
    dtin_bf = kb.sb("p1_dtin", [64, LC], BF16)
    BC_bf = kb.sb("p1_BCbf", [32, LC], BF16)
    bcBC = kb.sb("p1_bcBC", [128, 32 * LC], BF16)
    LCP = LC + 1
    sp1 = kb.sb("p1_sp1", [128, 2 * LC], F32)
    sp2 = kb.sb("p1_sp2", [128, 2 * LC], F32)
    u_t = kb.sb("p1_ut", [128, 2 * LC], F32)
    a16 = kb.sb("p1_a16", [128, 2 * NS * LCP], BF16)
    b16 = kb.sb("p1_b16", [128, NS * LCP], BF16)
    h16 = kb.sb("p1_h16", [128, NS * LCP], BF16)
    wring = kb.sb("p1_wring", [128, 4 * 512], BF16)
    yacc = kb.sb("p1_yacc", [128, LC], F32)
    ygate = kb.sb("p1_ygate", [128, NDT * LC], BF16)
    state = kb.sb("p1_state", [128, NDT * NS], F32)

    p_in = kb.psum("p1_pin", [128, 512], F32)
    p_in2 = kb.psum("p1_pin2", [128, 512], F32)
    p_xdbl = kb.psum("p1_pxdbl", [128, LC], F32)
    p_stat = kb.psum("p1_pstat", [1, 2 * LC], F32)
    p_bc = kb.psum("p1_pbc", [128, 2 * LC], F32)
    p_sel = kb.psum("p1_psel", [128, 256], F32)
    p_sel2 = kb.psum("p1_psel2", [128, 256], F32)

    xin_f = xin.rearrange("b l d -> (b l) d")
    xd_view = xd_hbm.rearrange("(t p) T -> p t T", p=128)
    zm_view = zm_hbm.rearrange("(t p) T -> p t T", p=128)

    kb.v(nc.vector.memset(a16[:, :], 0.0))
    for b in range(NB):
        kb.v(nc.vector.memset(state[:, :], 0.0))
        for ci in range(NCH):
            toff = b * L + ci * LC
            nc.vector.wait_ge(kb.Ss, kb.Sn)
            for dt in range(NDT):
                if ci == 0:
                    kb.v(nc.vector.memset(dv(xcin, dt, LC + 4, 0, 3), 0.0))
                else:
                    kb.v(nc.vector.tensor_copy(dv(xcin, dt, LC + 4, 0, 3),
                                               dv(xcin, dt, LC + 4, LC, LC + 3)))
            if ci == 0 and b == 0:
                kb.wait_tok(nc.tensor, t_wts)
            kb.bank_wait(nc.vector, "xdbuf")
            for tt in range(2):
                kb.bank_wait(nc.sync, "xstg")
                kb.sydma(xstage[:, :], xin_f[toff + tt * 128: toff + (tt + 1) * 128, :])
                t_x = kb.dma_mark()
                kb.wait_tok(nc.tensor, t_x)
                for grp in range(2):
                    pp = (p_in if grp % 2 == 0 else p_in2)
                    bk = "p1p%d" % (grp % 2)
                    kb.bank_wait(nc.tensor, bk)
                    pe_t = 0
                    for q in range(4):
                        ds = grp * 4 + q
                        pe_t = kb.pe_inc(nc.tensor.transpose(
                            pp[:, q * 128:(q + 1) * 128],
                            xstage[:, ds * 128:(ds + 1) * 128],
                            S["id_f"][:, :]))
                    if grp == 1:
                        kb.bank_set("xstg", [(kb.PEs, kb.PEn)])
                    nc.vector.wait_ge(kb.PEs, pe_t)
                    dst = xd[:, :].rearrange("p (ds t n) -> p ds t n", ds=8, t=2)[
                        :, grp * 4:(grp + 1) * 4, tt, :]
                    v_e = kb.v(nc.vector.tensor_copy(
                        dst, pp[:, 0:512].rearrange("p (a n) -> p a n", a=4)))
                    kb.bank_set(bk, [(kb.Vs, v_e)])
            v_xd = kb.Vn
            nc.sync.wait_ge(kb.Vs, v_xd)
            t_xd = kb.sydma(xd_view[:, :, toff:toff + LC],
                            xd[:, :].rearrange("p (t n) -> p t n", n=LC))

            pe_bc, mu_bc, rs_bc = _stats(nc, kb, S, xd, sq, stat,
                                         p_stat[0:1, 0:LC], p_stat[0:1, LC:2 * LC],
                                         p_bc[:, 0:LC], p_bc[:, LC:2 * LC],
                                         NDM, LC, S["epsln"][:, :], False, v_xd)
            nc.vector.wait_ge(kb.PEs, pe_bc)
            kb.bank_wait(nc.vector, "zlnbuf")
            kb.bank_wait(nc.vector, "zmstore")
            for t in range(NDM):
                kb.v(nc.vector.tensor_sub(tmp8[:, :], dv(xd, t, LC), mu_bc))
                kb.v(nc.vector.tensor_mul(dv(zln, t, LC), tmp8[:, :], rs_bc))
            v_zln = kb.Vn

            nc.tensor.wait_ge(kb.Vs, v_zln)
            kb.bank_wait(nc.scalar, "xzsink")
            for m in range(32):
                pp = (p_in if m % 2 == 0 else p_in2)
                bk = "p1p%d" % (m % 2)
                kb.bank_wait(nc.tensor, bk)
                pe_m = kb.mmgrp(pp[:, 0:LC],
                                [(wTin[:, kt * 4096 + m * 128:kt * 4096 + (m + 1) * 128],
                                  dv(zln, kt, LC)) for kt in range(8)])
                if m < 16:
                    nc.scalar.wait_ge(kb.PEs, pe_m)
                    s_e = kb.s(nc.scalar.activation(dv(xcin, m, LC + 4, 3, LC + 3),
                                                    pp[:, 0:LC], AF.Identity))
                    kb.bank_set(bk, [(kb.Ss, s_e)])
                else:
                    nc.scalar.wait_ge(kb.PEs, pe_m)
                    s_e = kb.s(nc.scalar.activation(dv(z_sil, m - 16, LC), pp[:, 0:LC], AF.Silu))
                    kb.bank_set(bk, [(kb.Ss, s_e)])

            kb.bank_set("zlnbuf", [(kb.PEs, kb.PEn)])
            nc.vector.wait_ge(kb.Ss, kb.Sn)
            kb.bank_wait(nc.scalar, "xcbf")
            silu_tok = [0, 0]
            for dt in range(NDT):
                cv = cvt[:, (dt % 2) * LC:(dt % 2 + 1) * LC]
                if silu_tok[dt % 2]:
                    nc.vector.wait_ge(kb.Ss, silu_tok[dt % 2])
                kb.v(nc.vector.tensor_scalar(cv, dv(xcin, dt, LC + 4, 0, LC),
                                             S["convw"][:, dt * 4:dt * 4 + 1], None, ALU.mult))
                for kk in range(1, 4):
                    kb.v(nc.vector.scalar_tensor_tensor(
                        cv, dv(xcin, dt, LC + 4, kk, LC + kk),
                        S["convw"][:, dt * 4 + kk:dt * 4 + kk + 1], cv,
                        ALU.mult, ALU.add))
                v_cv = kb.Vn
                nc.scalar.wait_ge(kb.Vs, v_cv)
                silu_tok[dt % 2] = kb.s(nc.scalar.activation(
                    dv(xc_bf, dt, LC), cv, AF.Silu, bias=S["convb"][:, dt:dt + 1]))
            s_xc = kb.Sn

            nc.tensor.wait_ge(kb.Ss, s_xc)
            kb.bank_wait(nc.tensor, "p1x")
            kb.bank_wait(nc.tensor, "dtinb")
            pe_xd = kb.mmgrp(p_xdbl[0:96, :],
                             [(dv(wTxp, kt, 96), dv(xc_bf, kt, LC)) for kt in range(NDT)])
            nc.vector.wait_ge(kb.PEs, pe_xd)
            kb.v(nc.vector.tensor_copy(dtin_bf[:, :], p_xdbl[0:64, :]))
            v_bc0 = kb.v(nc.vector.tensor_copy(BC_bf[:, :], p_xdbl[64:96, :]))
            kb.bank_set("p1x", [(kb.Vs, v_bc0)])
            kb.bank_set("xcbf", [(kb.PEs, kb.PEn)])

            nc.tensor.wait_ge(kb.Vs, v_bc0)
            kb.bank_wait(nc.scalar, "bcsink")
            for j in range(32):
                pp = (p_sel if j % 2 == 0 else p_sel2)[:, 0:LC]
                bk = "p1s%d" % (j % 2)
                kb.bank_wait(nc.tensor, bk)
                pe_j = kb.mmgrp(pp, [(S["sel32"][:, j * 128:(j + 1) * 128], BC_bf[:, :])])
                nc.scalar.wait_ge(kb.PEs, pe_j)
                s_e = kb.s(nc.scalar.activation(dv(bcBC, j, LC), pp, AF.Identity))
                kb.bank_set(bk, [(kb.Ss, s_e)])

            s_bc_done = kb.Sn
            nc.tensor.wait_ge(kb.Vs, v_bc0)
            slot_tree_v = [0, 0]
            slot_ab = [None, None]
            prev_scan_v = [0]
            def front(dt):
                sl = dt % 2
                sp1s = sp1[:, sl * LC:(sl + 1) * LC]
                sp2s = sp2[:, sl * LC:(sl + 1) * LC]
                uts = u_t[:, sl * LC:(sl + 1) * LC]
                a16s = a16[:, sl * NS * LCP:(sl + 1) * NS * LCP]
                pp = (p_in if sl == 0 else p_in2)
                bk = "p1p%d" % sl
                kb.bank_wait(nc.tensor, bk)
                pe_d = kb.mmgrp(pp[:, 0:LC], [(dv(wTdt, dt, 128), dtin_bf[:, :])])
                nc.scalar.wait_ge(kb.PEs, pe_d)
                if slot_tree_v[sl]:
                    nc.scalar.wait_ge(kb.Vs, slot_tree_v[sl])
                dtb_c = S["dtb"][:, dt:dt + 1]
                kb.s(nc.scalar.activation(sp1s, pp[:, 0:LC], AF.Relu, bias=dtb_c))
                s_sp = kb.s(nc.scalar.activation(sp2s, pp[:, 0:LC], AF.Abs, bias=dtb_c))
                kb.bank_set(bk, [(kb.Ss, s_sp)])
                kb.s(nc.scalar.activation(sp2s, sp2s, AF.Exp, scale=-1.0))
                s_sp = kb.s(nc.scalar.activation(sp2s, sp2s, AF.Ln, bias=S["ones_col_f"][:, :]))
                nc.gpsimd.wait_ge(kb.Ss, s_sp)
                if slot_tree_v[sl]:
                    nc.gpsimd.wait_ge(kb.Vs, slot_tree_v[sl])
                kb.g(nc.gpsimd.tensor_add(sp1s, sp1s, sp2s))
                g_dt = kb.g(nc.gpsimd.tensor_mul(uts, sp1s, dv(xc_bf, dt, LC)))
                nc.scalar.wait_ge(kb.Gs, g_dt)
                for n in range(NS):
                    kb.s(nc.scalar.activation(
                        a16s[:, n * LCP + 1:(n + 1) * LCP], sp1s, AF.Exp,
                        scale=S["Aneg"][:, dt * NS + n:dt * NS + n + 1]))
                slot_ab[sl] = [s_a := kb.Sn, None]

            def bterm(dt):
                sl = dt % 2
                uts = u_t[:, sl * LC:(sl + 1) * LC]
                if prev_scan_v[0]:
                    nc.gpsimd.wait_ge(kb.Vs, prev_scan_v[0])
                breal_w = b16[:, :].rearrange("p (n lp) -> p n lp", lp=LCP)[:, :, 1:LCP]
                g_tok = kb.g(nc.gpsimd.tensor_mul(
                    breal_w,
                    uts.unsqueeze(1).broadcast_to([128, NS, LC]),
                    bcBC[:, 0:NS * LC].rearrange("p (n l) -> p n l", n=NS)))
                slot_ab[sl][1] = g_tok

            def back(dt):
                sl = dt % 2
                a16s = a16[:, sl * NS * LCP:(sl + 1) * NS * LCP]
                areals = a16s.rearrange("p (n lp) -> p n lp", lp=LCP)[:, :, 1:LCP]
                breal_r = b16[:, :].rearrange("p (n lp) -> p n lp", lp=LCP)[:, :, 1:LCP]
                hreal = h16[:, :].rearrange("p (n lp) -> p n lp", lp=LCP)[:, :, 1:LCP]
                s_a, g_tok = slot_ab[sl]
                nc.vector.wait_ge(kb.Gs, g_tok)
                kb.v(nc.vector.tensor_copy(b16[:, 0:NS * LCP:LCP],
                                           state[:, dt * NS:(dt + 1) * NS]))
                nc.vector.wait_ge(kb.Ss, s_a)
                v_sc = kb.v(nc.vector.tensor_tensor_scan(
                    h16[:, :], a16s, b16[:, :], 0.0, ALU.mult, ALU.add))
                prev_scan_v[0] = v_sc
                if dt + 1 < NDT:
                    bterm(dt + 1)
                kb.v(nc.vector.tensor_copy(state[:, dt * NS:(dt + 1) * NS],
                                           h16[:, LC:NS * LCP:LCP]))
                kb.v(nc.vector.tensor_mul(
                    areals, hreal,
                    bcBC[:, NS * LC:2 * NS * LC].rearrange("p (n l) -> p n l", n=NS)))
                kb.v(nc.vector.tensor_add(areals[:, 0:8, :], areals[:, 0:8, :],
                                          areals[:, 8:16, :]))
                kb.v(nc.vector.tensor_add(areals[:, 0:4, :], areals[:, 0:4, :],
                                          areals[:, 4:8, :]))
                kb.v(nc.vector.tensor_add(areals[:, 0:2, :], areals[:, 0:2, :],
                                          areals[:, 2:4, :]))
                kb.v(nc.vector.tensor_add(yacc[:, :], areals[:, 0, :], areals[:, 1, :]))
                if dt == 0:
                    kb.bank_wait(nc.vector, "ygateb")
                kb.v(nc.vector.scalar_tensor_tensor(
                    yacc[:, :], dv(xc_bf, dt, LC), S["Dvec"][:, dt:dt + 1], yacc[:, :],
                    ALU.mult, ALU.add))
                v_g = kb.v(nc.vector.tensor_mul(dv(ygate, dt, LC), yacc[:, :],
                                                dv(z_sil, dt, LC)))
                slot_tree_v[sl] = v_g

            nc.vector.wait_ge(kb.Ss, s_bc_done)
            front(0)
            bterm(0)
            for dt in range(NDT):
                if dt + 1 < NDT:
                    front(dt + 1)
                back(dt)
            v_yg = kb.Vn
            kb.bank_set("xzsink", [(kb.Vs, v_yg)])
            kb.bank_set("bcsink", [(kb.Gs, kb.Gn), (kb.Vs, v_yg)])
            kb.bank_set("dtinb", [(kb.PEs, kb.PEn)])

            kb.wait_tok(nc.vector, t_xd)
            nc.tensor.wait_ge(kb.Vs, v_yg)
            banks4 = [p_in[:, 0:LC], p_in2[:, 0:LC], p_xdbl[:, 0:LC], p_sel[:, 0:LC]]
            bkeys4 = ["p1p0", "p1p1", "p1x", "p1s0"]
            for mp in range(2):
                for bk in bkeys4:
                    kb.bank_wait(nc.tensor, bk)
                for kt in range(NDT):
                    r = kt % 4
                    kb.bank_wait(nc.sync, "p1wr%d" % r)
                    tok = kb.trdma(wring[:, r * 512:(r + 1) * 512],
                                   wbf["out_proj"][mp * 512:(mp + 1) * 512,
                                                   kt * 128:(kt + 1) * 128])
                    kb.wait_tok(nc.tensor, tok)
                    for mi in range(4):
                        last = nc.tensor.matmul(
                            banks4[mi], wring[:, r * 512 + mi * 128:r * 512 + (mi + 1) * 128],
                            dv(ygate, kt, LC), start=(kt == 0), stop=(kt == NDT - 1))
                    kb.bank_set("p1wr%d" % r, [(kb.PEs, kb.pe_inc(last))])
                nc.vector.wait_ge(kb.PEs, kb.PEn)
                for mi in range(4):
                    m = mp * 4 + mi
                    v_e = kb.v(nc.vector.tensor_add(dv(xd, m, LC), banks4[mi], dv(zln, m, LC)))
                    kb.bank_set(bkeys4[mi], [(kb.Vs, v_e)])
            v_m1 = kb.Vn
            kb.bank_set("ygateb", [(kb.PEs, kb.PEn)])

            pe_bc2, _, rs2 = _stats(nc, kb, S, xd, sq, stat,
                                    p_stat[0:1, 0:LC], p_stat[0:1, LC:2 * LC],
                                    p_bc[:, 0:LC], p_bc[:, LC:2 * LC],
                                    NDM, LC, S["epsrms"][:, :], True, v_m1)
            nc.vector.wait_ge(kb.PEs, pe_bc2)
            kb.bank_wait(nc.vector, "zmstore")
            for t in range(NDM):
                kb.v(nc.vector.tensor_mul(tmp8[:, :], dv(xd, t, LC), rs2))
                kb.v(nc.vector.tensor_scalar(dv(zln, t, LC), tmp8[:, :],
                                             S["n1w"][:, t:t + 1], None, ALU.mult))
            v_zm = kb.Vn
            kb.bank_set("xdbuf", [(kb.PEs, kb.PEn)])
            nc.sync.wait_ge(kb.Vs, v_zm)
            t_zm = kb.sydma(zm_view[:, :, toff:toff + LC],
                            zln[:, :].rearrange("p (t n) -> p t n", n=LC))
            kb.bank_set("zmstore", [t_zm])



def _phase2(nc, kb, S, w, wbf, xd_hbm, zm_hbm, x1_hbm):
    with ExitStack() as ctx:
        kb_ctx, kb.ctx = kb.ctx, ctx
        try:
            _phase2_body(nc, kb, S, wbf, xd_hbm, zm_hbm, x1_hbm)
        finally:
            kb.barrier()
            kb.ctx = kb_ctx


def _phase2_body(nc, kb, S, wbf, xd_hbm, zm_hbm, x1_hbm):
    TS = 256
    NSB = TOK // TS
    wT1 = kb.sb("p2_wT1", [128, 8 * DFF], BF16)
    wT2 = kb.sb("p2_wT2", [128, 32 * D], BF16)
    kb.wait_gq(nc.sync, "mlp")
    for kt in range(8):
        kb.trdma(wT1[:, kt * DFF:(kt + 1) * DFF], wbf["mlp_w1"][:, kt * 128:(kt + 1) * 128])
    for kt in range(32):
        kb.trdma(wT2[:, kt * D:(kt + 1) * D], wbf["mlp_w2"][:, kt * 128:(kt + 1) * 128])
    t_wts = kb.dma_mark()

    zm_sub = kb.sb("p2_zm", [128, NDM * TS], BF16)
    xr_sub = kb.sb("p2_xr", [128, NDM * TS], F32)
    h1 = kb.sb("p2_h1", [128, 32 * TS], BF16)
    m1 = kb.sb("p2_m1", [128, NDM * TS], F32)
    sq = kb.sb("p2_sq", [128, NDM * TS], F32)
    stat = kb.sb("p2_stat", [1, 3 * TS], F32)
    tmp = kb.sb("p2_tmp", [128, TS], F32)
    pb = [kb.psum("p2_pb%d" % i, [128, 256], F32) for i in range(8)]

    xd_view = xd_hbm.rearrange("(t p) T -> p t T", p=128)
    zm_view = zm_hbm.rearrange("(t p) T -> p t T", p=128)
    x1_view = x1_hbm.rearrange("(t p) T -> p t T", p=128)

    kb.wait_tok(nc.tensor, t_wts)
    for sub in range(NSB):
        toff = sub * TS
        kb.bank_wait(nc.sync, "p2ld")
        kb.sydma(zm_sub[:, :].rearrange("p (t n) -> p t n", n=TS),
                 zm_view[:, :, toff:toff + TS])
        kb.sydma(xr_sub[:, :].rearrange("p (t n) -> p t n", n=TS),
                 xd_view[:, :, toff:toff + TS])
        t_ld = kb.dma_mark()
        kb.wait_tok(nc.tensor, t_ld)
        for ht in range(32):
            pp = pb[ht % 2]
            bk = "p2b%d" % (ht % 2)
            kb.bank_wait(nc.tensor, bk)
            pe_h = kb.mmgrp(pp[:, :],
                            [(wT1[:, kt * DFF + ht * 128:kt * DFF + (ht + 1) * 128],
                              dv(zm_sub, kt, TS)) for kt in range(8)])
            nc.scalar.wait_ge(kb.PEs, pe_h)
            s_e = kb.s(nc.scalar.activation(dv(h1, ht, TS), pp[:, :], AF.Gelu,
                                            bias=S["mlpb1"][:, ht:ht + 1]))
            kb.bank_set(bk, [(kb.Ss, s_e)])
        s_h1 = kb.Sn
        nc.tensor.wait_ge(kb.Ss, s_h1)
        pes = []
        for m in range(NDM):
            kb.bank_wait(nc.tensor, "p2b%d" % m)
            pes.append(kb.mmgrp(pb[m][:, :],
                                [(wT2[:, kt * D + m * 128:kt * D + (m + 1) * 128],
                                  dv(h1, kt, TS)) for kt in range(32)]))
        for m in range(NDM):
            nc.vector.wait_ge(kb.PEs, pes[m])
            v_e = kb.v(nc.vector.scalar_tensor_tensor(
                dv(m1, m, TS), pb[m][:, :], S["mlpb2"][:, m:m + 1], dv(zm_sub, m, TS),
                ALU.add, ALU.add))
            kb.bank_set("p2b%d" % m, [(kb.Vs, v_e)])
        v_m1 = kb.Vn
        for i in range(4):
            kb.bank_wait(nc.tensor, "p2b%d" % i)
        pe_bc, _, rs = _stats(nc, kb, S, m1, sq, stat,
                              pb[0][0:1, :], pb[1][0:1, :], pb[2][:, :], pb[3][:, :],
                              NDM, TS, S["epsrms"][:, :], True, v_m1)
        nc.vector.wait_ge(kb.PEs, pe_bc)
        kb.wait_tok(nc.vector, t_ld)
        for m in range(NDM):
            kb.v(nc.vector.tensor_mul(tmp[:, :], dv(m1, m, TS), rs))
            kb.v(nc.vector.scalar_tensor_tensor(
                dv(xr_sub, m, TS), tmp[:, :], S["n1w"][:, m:m + 1], dv(xr_sub, m, TS),
                ALU.mult, ALU.add))
        v_x1 = kb.Vn
        for i in range(4):
            kb.bank_set("p2b%d" % i, [(kb.Vs, v_x1)])
        nc.sync.wait_ge(kb.Vs, v_x1)
        t_st = kb.sydma(x1_view[:, :, toff:toff + TS],
                        xr_sub[:, :].rearrange("p (t n) -> p t n", n=TS))
        kb.bank_set("p2ld", [(kb.PEs, kb.PEn), t_st])


def _phase_ffn(nc, kb, S, w, wbf, xs_hbm, xo_hbm, out_dram, tag):
    with ExitStack() as ctx:
        kb_ctx, kb.ctx = kb.ctx, ctx
        try:
            _ffn_body(nc, kb, S, wbf, xs_hbm, xo_hbm, out_dram, tag)
        finally:
            kb.barrier()
            kb.ctx = kb_ctx


def _ffn_body(nc, kb, S, wbf, xs_hbm, xo_hbm, out_dram, tag):
    TH = 1024
    CH = 128
    T5 = 512
    xacc = kb.sb(tag + "_xacc", [128, NDM * TH], F32)
    xln = kb.sb(tag + "_xln", [128, NDM * TH], BF16)
    w1eT = kb.sb(tag + "_w1eT", [128, 2 * 8 * DH], BF16)
    w2eT = kb.sb(tag + "_w2eT", [128, 16 * D], BF16)
    h1 = kb.sb(tag + "_h1", [128, 16 * T5], BF16)
    cbc = kb.sb(tag + "_cbc", [128, T5], BF16)
    hstage = kb.sb(tag + "_hstg", [128, 2 * T5], BF16)
    ostage = kb.sb(tag + "_ostg", [128, 384], F32) if out_dram is not None else None
    combT = kb.sb(tag + "_combT", [8, TH], BF16)
    g_sb = kb.sb(tag + "_gsb", [8, T5], BF16)
    gt = kb.sb(tag + "_gt", [128, 8], F32)
    gw = kb.sb(tag + "_gw", [128, 8], F32)
    cmb = kb.sb(tag + "_cmb", [128, 8], F32)
    cmb_bf = kb.sb(tag + "_cmbbf", [128, 8], BF16)
    col1 = kb.sb(tag + "_col1", [128, 4], F32)
    wg_sb = kb.sb(tag + "_wgsb", [8, D], BF16)
    wgT = kb.sb(tag + "_wgT", [128, 8 * 8], BF16)
    b2sb = kb.sb(tag + "_b2sb", [8, D], BF16)
    sq = kb.sb(tag + "_sq", [128, NDM * CH], BF16)
    stat = kb.sb(tag + "_stat", [1, 3 * CH], F32)
    tmp = kb.sb(tag + "_tmp", [128, CH], F32)

    pb = [kb.psum(tag + "_pb%d" % i, [128, 512], F32) for i in range(7)]
    ptr8 = kb.psum(tag + "_ptr8", [128, 128], BF16)
    ptrf = pb[6]

    xs_view = xs_hbm.rearrange("(t p) T -> p t T", p=128)
    if xo_hbm is not None:
        xo_view = xo_hbm.rearrange("(t p) T -> p t T", p=128)

    kb.wait_gq(nc.sync, "ffn")
    kb.sydma(wg_sb[:, :], wbf["Wg"][:, :])
    kb.sydma(b2sb[:, :], wbf["b2"][:, :])
    t_g = kb.dma_mark()
    kb.wait_tok(nc.tensor, t_g)
    for kt in range(8):
        pe_t = kb.pe_inc(nc.tensor.transpose(ptr8[:, (kt % 2) * 8:(kt % 2) * 8 + 8],
                                             wg_sb[:, kt * 128:(kt + 1) * 128],
                                             S["id_bf"][0:8, 0:8]))
        nc.vector.wait_ge(kb.PEs, pe_t)
        kb.v(nc.vector.tensor_copy(wgT[:, kt * 8:(kt + 1) * 8],
                                   ptr8[:, (kt % 2) * 8:(kt % 2) * 8 + 8]))
        nc.tensor.wait_ge(kb.Vs, kb.Vn)

    for half in range(TOK // TH):
        hoff = half * TH
        kb.bank_wait(nc.sync, tag + "qld")
        t_x = kb.sydma(xacc[:, :].rearrange("p (t n) -> p t n", n=TH),
                       xs_view[:, :, hoff:hoff + TH])
        kb.wait_tok(nc.vector, t_x)
        kb.bank_wait(nc.vector, tag + "qld")
        kb.bank_wait(nc.scalar, tag + "qld")
        v_x = kb.v(nc.vector.tensor_copy(xacc[:, 0:1], xacc[:, 0:1]))
        for c in range(TH // CH):
            tiles = [dv(xacc, t, TH, c * CH, (c + 1) * CH) for t in range(NDM)]
            pe_bc, mu_bc, rs_bc = _stats(nc, kb, S, None, sq, stat,
                                         pb[0][0:1, 0:CH], pb[1][0:1, 0:CH],
                                         pb[2][:, 0:CH], pb[3][:, 0:CH],
                                         NDM, CH, S["epsln"][:, :], False, v_x,
                                         src_tiles=tiles)
            nc.vector.wait_ge(kb.PEs, pe_bc)
            for t in range(NDM):
                kb.v(nc.vector.tensor_sub(tmp[:, :], tiles[t], mu_bc))
                kb.v(nc.vector.tensor_mul(dv(xln, t, TH, c * CH, (c + 1) * CH),
                                          tmp[:, :], rs_bc))
        v_xln = kb.Vn

        nc.tensor.wait_ge(kb.Vs, v_xln)
        for c5 in range(TH // T5):
            pe_g = kb.mmgrp(pb[4][0:8, 0:T5],
                            [(wgT[:, kt * 8:(kt + 1) * 8],
                              dv(xln, kt, TH, c5 * T5, (c5 + 1) * T5)) for kt in range(NDM)])
            nc.scalar.wait_ge(kb.PEs, pe_g)
            s_g = kb.s(nc.scalar.activation(g_sb[:, :], pb[4][0:8, 0:T5], AF.Identity,
                                            bias=S["bg8"][:, :]))
            for i in range(T5 // 128):
                nc.tensor.wait_ge(kb.Ss, s_g)
                nc.tensor.wait_ge(kb.Vs, kb.Vn)
                pe_t = kb.pe_inc(nc.tensor.transpose(
                    ptr8[:, 8:16], g_sb[:, i * 128:(i + 1) * 128], S["id_bf"][0:8, 0:8]))
                nc.vector.wait_ge(kb.PEs, pe_t)
                kb.v(nc.vector.tensor_copy(gt[:, :], ptr8[:, 8:16]))
                kb.v(nc.vector.reduce_max(col1[:, 0:1], gt[:, :], axis=AX.X))
                kb.v(nc.vector.tensor_scalar(cmb[:, :], gt[:, :], col1[:, 0:1], None, ALU.is_equal))
                kb.v(nc.vector.scalar_tensor_tensor(gw[:, :], cmb[:, :], -1e30, gt[:, :],
                                                    ALU.mult, ALU.add))
                kb.v(nc.vector.reduce_max(col1[:, 1:2], gw[:, :], axis=AX.X))
                kb.v(nc.vector.tensor_scalar(gw[:, :], gw[:, :], col1[:, 1:2], None, ALU.is_equal))
                v_d = kb.v(nc.vector.tensor_sub(col1[:, 2:3], col1[:, 0:1], col1[:, 1:2]))
                nc.scalar.wait_ge(kb.Vs, v_d)
                kb.s(nc.scalar.activation(col1[:, 3:4], col1[:, 2:3], AF.Sigmoid))
                s_s2 = kb.s(nc.scalar.activation(col1[:, 2:3], col1[:, 2:3], AF.Sigmoid,
                                                 scale=-1.0))
                nc.vector.wait_ge(kb.Ss, s_s2)
                kb.v(nc.vector.tensor_scalar(cmb[:, :], cmb[:, :], col1[:, 3:4], None, ALU.mult))
                kb.v(nc.vector.scalar_tensor_tensor(cmb[:, :], gw[:, :], col1[:, 2:3], cmb[:, :],
                                                    ALU.mult, ALU.add))
                v_c = kb.v(nc.vector.tensor_copy(cmb_bf[:, :], cmb[:, :]))
                nc.tensor.wait_ge(kb.Vs, v_c)
                pe_t2 = kb.pe_inc(nc.tensor.transpose(ptr8[0:8, 0:128], cmb_bf[:, :], S["id_bf"][:, :]))
                nc.vector.wait_ge(kb.PEs, pe_t2)
                kb.v(nc.vector.tensor_copy(combT[:, c5 * T5 + i * 128:c5 * T5 + (i + 1) * 128],
                                           ptr8[0:8, 0:128]))
        v_comb = kb.Vn

        def load_w1(e, slot):
            kb.bank_wait(nc.sync, tag + "w1s%d" % slot)
            base = slot * 8 * DH
            for kt in range(8):
                kb.trdma(w1eT[:, base + kt * DH:base + (kt + 1) * DH],
                         wbf["W1"][e][:, kt * 128:(kt + 1) * 128])
            return kb.dma_mark()

        def load_w2(e):
            kb.bank_wait(nc.sync, tag + "w2s")
            for kt in range(16):
                kb.trdma(w2eT[:, kt * D:(kt + 1) * D],
                         wbf["W2"][e][:, kt * 128:(kt + 1) * 128])
            return kb.dma_mark()

        w1_tok = [None, None]
        w1_tok[0] = load_w1(0, 0)
        w2_tok = load_w2(0)
        for e in range(NE):
            if e < NE - 1:
                w1_tok[(e + 1) % 2] = load_w1(e + 1, (e + 1) % 2)
            kb.wait_tok(nc.tensor, w1_tok[e % 2])
            w1v = w1eT[:, (e % 2) * 8 * DH:(e % 2 + 1) * 8 * DH]
            for c5 in range(TH // T5):
                sl5 = (c5 * T5, (c5 + 1) * T5)
                kb.bank_wait(nc.tensor, tag + "pb4")
                pe_c = kb.mmgrp(pb[4][:, :],
                                [(S["sel32"][0:8, e * 128:(e + 1) * 128],
                                  combT[:, sl5[0]:sl5[1]])])
                nc.vector.wait_ge(kb.PEs, pe_c)
                v_e = kb.v(nc.vector.tensor_copy(cbc[:, :], pb[4][:, :]))
                kb.bank_set(tag + "pb4", [(kb.Vs, v_e)])
                hrot = (0, 1, 6)
                for ht in range(16):
                    bi = hrot[ht % 3]
                    bk = tag + "pb%d" % bi
                    kb.bank_wait(nc.tensor, bk)
                    pe_h = kb.mmgrp(pb[bi][:, :],
                                    [(w1v[:, kt * DH + ht * 128:kt * DH + (ht + 1) * 128],
                                      dv(xln, kt, TH, sl5[0], sl5[1])) for kt in range(NDM)])
                    hs = hstage[:, (ht % 2) * T5:(ht % 2 + 1) * T5]
                    nc.scalar.wait_ge(kb.PEs, pe_h)
                    kb.bank_wait(nc.scalar, tag + "hstg%d" % (ht % 2))
                    s_e = kb.s(nc.scalar.activation(
                        hs, pb[bi][:, :], AF.Relu,
                        bias=S["b1c"][:, e * 16 + ht:e * 16 + ht + 1]))
                    kb.bank_set(bk, [(kb.Ss, s_e)])
                    nc.vector.wait_ge(kb.Ss, s_e)
                    v_e = kb.v(nc.vector.tensor_mul(dv(h1, ht, T5), hs, cbc[:, :]))
                    kb.bank_set(tag + "hstg%d" % (ht % 2), [(kb.Vs, v_e)])
                v_h = kb.Vn
                nc.tensor.wait_ge(kb.Vs, v_h)
                kb.wait_tok(nc.tensor, w2_tok)
                rot = (2, 3, 5)
                for m in range(NDM):
                    bi = rot[m % 3]
                    bk = tag + "pb%d" % bi
                    kb.bank_wait(nc.tensor, bk)
                    pe_m = kb.mmgrp(pb[bi][:, :],
                                    [(w2eT[:, kt * D + m * 128:kt * D + (m + 1) * 128],
                                      dv(h1, kt, T5)) for kt in range(16)])
                    nc.vector.wait_ge(kb.PEs, pe_m)
                    v_e = kb.v(nc.vector.tensor_add(dv(xacc, m, TH, sl5[0], sl5[1]),
                                                    pb[bi][:, :],
                                                    dv(xacc, m, TH, sl5[0], sl5[1])))
                    kb.bank_set(bk, [(kb.Vs, v_e)])
            kb.bank_set(tag + "w1s%d" % (e % 2), [(kb.PEs, kb.PEn)])
            kb.bank_set(tag + "w2s", [(kb.PEs, kb.PEn)])
            if e < NE - 1:
                w2_tok = load_w2(e + 1)
        for c5 in range(TH // T5):
            sl5 = (c5 * T5, (c5 + 1) * T5)
            for m in range(NDM):
                bk = tag + "pb%d" % (4 + m % 2)
                kb.bank_wait(nc.tensor, bk)
                pe_b = kb.mmgrp(pb[4 + (m % 2)][:, :], [(b2sb[:, m * 128:(m + 1) * 128],
                                                         combT[:, sl5[0]:sl5[1]])])
                nc.vector.wait_ge(kb.PEs, pe_b)
                v_e = kb.v(nc.vector.tensor_add(dv(xacc, m, TH, sl5[0], sl5[1]),
                                                pb[4 + (m % 2)][:, :],
                                                dv(xacc, m, TH, sl5[0], sl5[1])))
                kb.bank_set(bk, [(kb.Vs, v_e)])
        v_out = kb.Vn

        if out_dram is None:
            nc.sync.wait_ge(kb.Vs, v_out)
            t_sq = kb.sydma(xo_view[:, :, hoff:hoff + TH],
                            xacc[:, :].rearrange("p (t n) -> p t n", n=TH))
            kb.bank_set(tag + "qld", [(kb.PEs, kb.PEn), t_sq])
        else:
            of = out_dram.rearrange("b l d -> (b l) d")
            slot_tok = [None] * 3
            si = 0
            for ti in range(TH // 128):
                for m in range(NDM):
                    sl = ostage[:, (si % 3) * 128:(si % 3) * 128 + 128]
                    nc.tensor.wait_ge(kb.Vs, kb.Vn)
                    pe_t = kb.pe_inc(nc.tensor.transpose(
                        ptrf[:, 0:128], dv(xacc, m, TH, ti * 128, (ti + 1) * 128),
                        S["id_f"][:, :]))
                    nc.vector.wait_ge(kb.PEs, pe_t)
                    if slot_tok[si % 3] is not None:
                        kb.wait_tok(nc.vector, slot_tok[si % 3])
                    v_st = kb.v(nc.vector.tensor_copy(sl, ptrf[:, 0:128]))
                    nc.sync.wait_ge(kb.Vs, v_st)
                    slot_tok[si % 3] = kb.sydma(
                        of[hoff + ti * 128: hoff + (ti + 1) * 128, m * 128:(m + 1) * 128], sl)
                    si += 1
            kb.bank_set(tag + "qld", [(kb.PEs, kb.PEn)] + [t for t in slot_tok if t])


def _phase4(nc, kb, S, w, wbf, x2_hbm, x3_hbm):
    with ExitStack() as ctx:
        kb_ctx, kb.ctx = kb.ctx, ctx
        try:
            _phase4_body(nc, kb, S, wbf, x2_hbm, x3_hbm)
        finally:
            kb.barrier()
            kb.ctx = kb_ctx


def _phase4_body(nc, kb, S, wbf, x2_hbm, x3_hbm):
    CH = 256
    xact = kb.sb("p4_xact", [128, NDM * L], BF16)
    stb = kb.sb("p4_stb", [128, NDM * L], BF16)
    ybuf = kb.sb("p4_y", [128, NDM * L], F32)
    wTW = kb.sb("p4_wTW", [128, NDM * D], BF16)
    x2c = kb.sb("p4_x2c", [128, NDM * CH], F32)
    sq = kb.sb("p4_sq", [128, NDM * CH], F32)
    stat = kb.sb("p4_stat", [1, 3 * CH], F32)
    tmp = kb.sb("p4_tmp", [128, CH], F32)
    tmp2 = kb.sb("p4_tmp2", [128, CH], F32)
    Sst = kb.sb("p4_Sst", [128, 8 * 64], F32)
    Sbf = kb.sb("p4_Sbf", [128, 64], BF16)
    MT = kb.sb("p4_MT", [128, 2 * 512], BF16)
    kv = kb.sb("p4_kv", [128, 2 * 256], BF16)
    rhs0 = kb.sb("p4_rhs0", [128, 4 * 64], F32)
    Xb = kb.sb("p4_X", [128, 4 * 64], F32)
    Xbf = kb.sb("p4_Xbf", [128, 4 * 64], BF16)
    odyn = kb.sb("p4_odyn", [128, 4 * 64], BF16)
    wsc = kb.sb("p4_wsc", [128, 4 * 64], BF16)
    kq = kb.sb("p4_kq", [64, 2 * CH], F32)
    ksr = kb.sb("p4_ksr", [1, 2 * CH], F32)
    ccol = kb.sb("p4_ccol", [128, 4], F32)
    dst = kb.sb("p4_dst", [64, 2 * 64], F32)
    pbank = [kb.psum("p4_pb%d" % i, [128, 512], F32) for i in range(6)]
    pbf = kb.psum("p4_pbf", [128, 256], BF16)
    pbf2 = kb.psum("p4_pbf2", [128, 256], BF16)

    x2_view = x2_hbm.rearrange("(t p) T -> p t T", p=128)
    x3_view = x3_hbm.rearrange("(t p) T -> p t T", p=128)

    kb.wait_gq(nc.sync, "wb")
    for kt in range(NDM):
        kb.trdma(wTW[:, kt * D:(kt + 1) * D], wbf["Wb"][:, kt * 128:(kt + 1) * 128])
    t_w = kb.dma_mark()
    kb.wait_tok(nc.tensor, t_w)

    def hslice(buf, h, ci, lo, hi):
        dth = h // 2
        pb = (h % 2) * 64
        return buf[pb:pb + 64, dth * L + ci * CH + lo: dth * L + ci * CH + hi]

    for b in range(NB):
        boff = b * L
        kb.v(nc.vector.memset(Sst[:, :], 0.0))
        for ci in range(NCH):
            toff = boff + ci * CH
            nc.sync.wait_ge(kb.Ss, kb.Sn)
            nc.sync.wait_ge(kb.Vs, kb.Vn)
            kb.sydma(x2c[:, :].rearrange("p (t n) -> p t n", n=CH),
                     x2_view[:, :, toff:toff + CH])
            t_x = kb.dma_mark()
            kb.wait_tok(nc.vector, t_x)
            v_x = kb.v(nc.vector.tensor_copy(x2c[:, 0:1], x2c[:, 0:1]))
            pe_bc, mu_bc, rs_bc = _stats(nc, kb, S, x2c, sq, stat,
                                         pbank[4][0:1, 0:CH], pbank[5][0:1, 0:CH],
                                         pbank[0][:, 0:CH], pbank[1][:, 0:CH],
                                         NDM, CH, S["epsln"][:, :], False, v_x)
            nc.vector.wait_ge(kb.PEs, pe_bc)
            for t in range(NDM):
                nc.vector.wait_ge(kb.Ss, kb.Sn)
                kb.v(nc.vector.tensor_sub(tmp[:, :], dv(x2c, t, CH), mu_bc))
                v_t = kb.v(nc.vector.tensor_mul(tmp2[:, :], tmp[:, :], rs_bc))
                nc.scalar.wait_ge(kb.Vs, v_t)
                kb.s(nc.scalar.activation(xact[:, t * L + ci * CH: t * L + (ci + 1) * CH],
                                          tmp2[:, :], AF.Tanh))
        s_xact = kb.Sn
        nc.tensor.wait_ge(kb.Ss, s_xact)
        for ci in range(NCH):
            for m in range(NDM):
                pp = pbank[m % 2]
                bk = "p4b%d" % (m % 2)
                kb.bank_wait(nc.tensor, bk)
                pe_m = kb.mmgrp(pp[:, 0:CH],
                                [(wTW[:, kt * D + m * 128:kt * D + (m + 1) * 128],
                                  xact[:, kt * L + ci * CH: kt * L + (ci + 1) * CH])
                                 for kt in range(NDM)])
                nc.vector.wait_ge(kb.PEs, pe_m)
                kb.v(nc.vector.tensor_copy(stb[:, m * L + ci * CH: m * L + (ci + 1) * CH],
                                           pp[:, 0:CH]))
                v_e = kb.v(nc.vector.tensor_scalar_mul(
                    ybuf[:, m * L + ci * CH: m * L + (ci + 1) * CH], pp[:, 0:CH], 2.0))
                kb.bank_set(bk, [(kb.Vs, v_e)])
        def delta_gen(slot, h, ci):
            pbh = (h % 2) * 64
            sblk = (h // 2) * 64
            bA = pbank[3 * slot + 0]
            bT = pbank[3 * slot + 1]
            bM = pbank[3 * slot + 2]
            pqf = pbf if slot == 0 else pbf2
            MTs = MT[:, slot * 512:(slot + 1) * 512]
            kvs = kv[:, slot * 256:(slot + 1) * 256]
            kts = None
            vts = None
            r0s = rhs0[:, slot * 128:(slot + 1) * 128]
            Xs = Xb[:, slot * 128:(slot + 1) * 128]
            Xfs = Xbf[:, slot * 128:(slot + 1) * 128]
            ods = odyn[:, slot * 128:(slot + 1) * 128]
            wss = wsc[:, slot * 128:(slot + 1) * 128]
            kqs = kq[:, slot * CH:(slot + 1) * CH]
            ksrs = ksr[:, slot * CH:(slot + 1) * CH]
            ccs = ccol[:, slot * 2:slot * 2 + 2]
            dsts = dst[:, slot * 64:(slot + 1) * 64]
            bkA = "p4A%d" % slot
            bkT = "p4T%d" % slot
            bkM = "p4M%d" % slot
            bkQ = "p4Q%d" % slot

            kb.bank_wait(nc.tensor, bkA)
            for mt in range(2):
                kb.mmgrp(bA[:, mt * 256:(mt + 1) * 256],
                         [(hslice(xact, h, ci, mt * 128, (mt + 1) * 128),
                           hslice(xact, h, ci, 0, 256))])
            pe_A = kb.PEn
            nc.scalar.wait_ge(kb.PEs, kb.PEn)
            kb.s(nc.scalar.activation(kqs[:, :], hslice(xact, h, ci, 0, 256), AF.Square))
            s_kq = kb.Sn
            nc.vector.wait_ge(kb.PEs, pe_A)
            kb.v(nc.vector.tensor_mul(MTs[:, 0:256], bA[:, 0:256], S["Upre0"][:, :]))
            v_mt = kb.v(nc.vector.tensor_mul(MTs[:, 256:512], bA[:, 256:512], S["Upre1"][:, :]))
            kb.bank_set(bkA, [(kb.Vs, v_mt)])
            yield

            nc.tensor.wait_ge(kb.Ss, s_kq)
            kb.bank_wait(nc.tensor, bkT)
            kb.mmgrp(bT[0:1, 128:384], [(S["ones_col_f"][0:64, :], kqs[:, :])])
            pe_kq = kb.PEn
            nc.vector.wait_ge(kb.PEs, pe_kq)
            v_ksr = kb.v(nc.vector.tensor_copy(ksrs[:, :], bT[0:1, 128:384]))
            nc.tensor.wait_ge(kb.Vs, v_ksr)
            for st in range(2):
                kb.pe_inc(nc.tensor.transpose(bT[:, 384 + st:384 + st + 1],
                                              ksrs[0:1, st * 128:(st + 1) * 128],
                                              S["id_f"][0:1, 0:1]))
            pe_cc = kb.PEn
            nc.vector.wait_ge(kb.PEs, pe_cc)
            for st in range(2):
                kb.v(nc.vector.tensor_scalar(ccs[:, st:st + 1], bT[:, 384 + st:384 + st + 1],
                                             S["beta_c"][:, :], 1.0, ALU.mult, ALU.subtract))
            v_cc = kb.v(nc.vector.tensor_copy(Sbf[pbh:pbh + 64, :],
                                              Sst[pbh:pbh + 64, sblk:sblk + 64]))
            kb.bank_set(bkT, [(kb.Vs, v_cc)])
            idb = S["id_bf"][pbh:pbh + 64, pbh:pbh + 64]
            for st in range(2):
                kb.bank_wait(nc.tensor, bkQ)
                kb.pe_inc(nc.tensor.transpose(
                    pqf[:, 0:64], hslice(xact, h, ci, st * 128, (st + 1) * 128), idb))
                kb.pe_inc(nc.tensor.transpose(
                    pqf[:, 64:128], hslice(stb, h, ci, st * 128, (st + 1) * 128), idb))
                nc.vector.wait_ge(kb.PEs, kb.PEn)
                kb.v(nc.vector.tensor_copy(kvs[:, st * 128:(st + 1) * 128], pqf[:, 0:128]))
                kb.bank_set(bkQ, [(kb.Vs, kb.Vn)])
            yield

            nc.tensor.wait_ge(kb.Vs, v_cc)
            kb.bank_wait(nc.tensor, bkT)
            for mt in range(2):
                kb.mmgrp(bT[:, mt * 64:(mt + 1) * 64],
                         [(hslice(xact, h, ci, mt * 128, (mt + 1) * 128),
                           Sbf[pbh:pbh + 64, :])])
            pe_T1 = kb.PEn
            nc.vector.wait_ge(kb.PEs, pe_T1)
            for st in range(2):
                kb.v(nc.vector.scalar_tensor_tensor(
                    r0s[:, st * 64:(st + 1) * 64], bT[:, st * 64:(st + 1) * 64],
                    S["nlamj1"][:, st:st + 1], kvs[:, st * 128 + 64:st * 128 + 128],
                    ALU.mult, ALU.add))
            v_X = kb.v(nc.vector.tensor_copy(Xfs[:, :], r0s[:, :]))
            kb.bank_set(bkT, [(kb.Vs, v_X)])
            yield

            for it in range(3):
                nc.tensor.wait_ge(kb.Vs, v_X)
                kb.bank_wait(nc.tensor, bkM)
                for tt in range(2):
                    kb.mmgrp(bM[:, tt * 64:(tt + 1) * 64],
                             [(MTs[:, 0 + tt * 128:0 + (tt + 1) * 128], Xfs[:, 0:64]),
                              (MTs[:, 256 + tt * 128:256 + (tt + 1) * 128], Xfs[:, 64:128])])
                pe_mx = kb.PEn
                nc.vector.wait_ge(kb.PEs, pe_mx)
                for tt in range(2):
                    kb.v(nc.vector.tensor_sub(Xs[:, tt * 64:(tt + 1) * 64],
                                              r0s[:, tt * 64:(tt + 1) * 64],
                                              bM[:, tt * 64:(tt + 1) * 64]))
                if it < 2:
                    v_X = kb.v(nc.vector.tensor_copy(Xfs[:, :], Xs[:, :]))
                kb.bank_set(bkM, [(kb.Vs, kb.Vn)])
                yield

            for st in range(2):
                kb.v(nc.vector.tensor_scalar(ods[:, st * 64:(st + 1) * 64],
                                             Xs[:, st * 64:(st + 1) * 64],
                                             ccs[:, st:st + 1], None, ALU.mult))
                kb.v(nc.vector.tensor_scalar(wss[:, st * 64:(st + 1) * 64],
                                             Xs[:, st * 64:(st + 1) * 64],
                                             S["brev"][:, st:st + 1], None, ALU.mult))
            v_ow = kb.Vn
            nc.tensor.wait_ge(kb.Vs, v_ow)
            kb.bank_wait(nc.tensor, bkQ)
            for st in range(2):
                kb.pe_inc(nc.tensor.transpose(pqf[0:64, st * 128:(st + 1) * 128],
                                              ods[:, st * 64:(st + 1) * 64],
                                              S["id_bf"][:, :]))
            nc.vector.wait_ge(kb.PEs, kb.PEn)
            kb.v(nc.vector.tensor_add(
                hslice(ybuf, h, ci, 0, 256), hslice(ybuf, h, ci, 0, 256),
                pqf[0:64, 0:256]))
            kb.bank_set(bkQ, [(kb.Vs, kb.Vn)])
            yield

            kb.bank_wait(nc.tensor, bkT)
            kb.mmgrp(bT[0:64, 128:192],
                     [(wss[:, 0:64], kvs[:, 0:64]), (wss[:, 64:128], kvs[:, 128:192])])
            pe_dst = kb.PEn
            nc.vector.wait_ge(kb.PEs, pe_dst)
            v_dst = kb.v(nc.vector.tensor_copy(dsts[:, :], bT[0:64, 128:192]))
            nc.tensor.wait_ge(kb.Vs, v_dst)
            kb.pe_inc(nc.tensor.transpose(bT[0:64, 192:256], dsts[:, :],
                                          S["id_f"][0:64, 0:64]))
            nc.vector.wait_ge(kb.PEs, kb.PEn)
            v_s = kb.v(nc.vector.scalar_tensor_tensor(
                Sst[pbh:pbh + 64, sblk:sblk + 64], Sst[pbh:pbh + 64, sblk:sblk + 64],
                S["lamQ_c"][pbh:pbh + 64, :], bT[0:64, 192:256], ALU.mult, ALU.add))
            kb.bank_set(bkT, [(kb.Vs, v_s)])

        _sent = object()
        nc.tensor.wait_ge(kb.Vs, kb.Vn)
        for pair in range(8):
            for ci in range(NQ):
                g0 = delta_gen(0, pair * 2, ci)
                g1 = delta_gen(1, pair * 2 + 1, ci)
                alive = [g0, g1]
                while alive:
                    for g in list(alive):
                        if next(g, _sent) is _sent:
                            alive.remove(g)
        prev_store = None
        for ci in range(NCH):
            toff = boff + ci * CH
            nc.sync.wait_ge(kb.Vs, kb.Vn)
            nc.sync.wait_ge(kb.Ss, kb.Sn)
            if prev_store is not None:
                kb.wait_tok(nc.sync, prev_store)
            kb.sydma(x2c[:, :].rearrange("p (t n) -> p t n", n=CH),
                     x2_view[:, :, toff:toff + CH])
            t_x = kb.dma_mark()
            kb.wait_tok(nc.vector, t_x)
            ytiles = [ybuf[:, t * L + ci * CH: t * L + (ci + 1) * CH] for t in range(NDM)]
            pe_bc, mu_bc, rs_bc = _stats(nc, kb, S, None, sq, stat,
                                         pbank[4][0:1, 0:CH], pbank[5][0:1, 0:CH],
                                         pbank[0][:, 0:CH], pbank[1][:, 0:CH],
                                         NDM, CH, S["epsln"][:, :], False, kb.Vn,
                                         src_tiles=ytiles)
            nc.vector.wait_ge(kb.PEs, pe_bc)
            for t in range(NDM):
                kb.v(nc.vector.tensor_sub(tmp[:, :], ytiles[t], mu_bc))
                kb.v(nc.vector.tensor_mul(tmp2[:, :], tmp[:, :], rs_bc))
                kb.v(nc.vector.scalar_tensor_tensor(
                    tmp[:, :], tmp2[:, :], S["lng"][:, t:t + 1], dv(x2c, t, CH),
                    ALU.mult, ALU.add))
                kb.v(nc.vector.tensor_scalar(dv(x2c, t, CH), tmp[:, :],
                                             S["lnb"][:, t:t + 1], None, ALU.add))
            v_x3 = kb.Vn
            nc.sync.wait_ge(kb.Vs, v_x3)
            prev_store = kb.sydma(x3_view[:, :, toff:toff + CH],
                                  x2c[:, :].rearrange("p (t n) -> p t n", n=CH))
        kb.barrier()



PHASES = (1, 2, 3, 4, 5)


def build_program(debug=False, phases=PHASES):
    nc = bass.Bass("TRN2", target_bir_lowering=False)
    xin = nc.dram_tensor("xin", [NB, L, D], F32, kind="ExternalInput")
    w = {}
    for name, shape in [
        ("in_proj", [2 * DI, D]), ("conv_w", [DI, 4]), ("conv_b", [DI]),
        ("x_proj", [DTR + 2 * NS, DI]), ("dt_proj", [DI, DTR]), ("dt_b", [DI]),
        ("A_log", [DI, NS]), ("Dv", [DI]), ("out_proj", [D, DI]),
        ("mlp_w1", [DFF, D]), ("mlp_b1", [DFF]), ("mlp_w2", [D, DFF]),
        ("mlp_b2", [D]), ("norm1_w", [D]),
        ("Wb", [D, D]), ("ln_g", [D]), ("ln_b", [D]), ("eta", [1, 1]), ("lam", [1, 1]),
        ("W1", [NE, DH, D]), ("b1", [NE, DH]), ("W2", [NE, D, DH]),
        ("b2", [NE, D]), ("Wg", [NE, D]), ("bg", [NE, 1]),
    ]:
        w[name] = nc.dram_tensor(name, shape, F32, kind="ExternalInput")
    out = nc.dram_tensor("out", [NB, L, D], F32, kind="ExternalOutput")

    sk = "ExternalOutput" if debug else "Internal"
    xd_hbm = nc.dram_tensor("xd_hbm", [D, TOK], F32, kind=sk)
    zm_hbm = nc.dram_tensor("zm_hbm", [D, TOK], BF16, kind=sk)
    x1_hbm = nc.dram_tensor("x1_hbm", [D, TOK], F32, kind=sk)
    x2_hbm = nc.dram_tensor("x2_hbm", [D, TOK], F32, kind=sk)
    x3_hbm = nc.dram_tensor("x3_hbm", [D, TOK], F32, kind=sk)

    wbf = {}
    for name in ["in_proj", "x_proj", "dt_proj", "out_proj",
                 "mlp_w1", "mlp_w2", "Wb", "W1", "W2", "Wg", "b2"]:
        wbf[name] = nc.dram_tensor(name + "_bf", list(w[name].shape), BF16, kind="Internal")

    with ExitStack() as ctx:
        kb = KB(nc, ctx)
        S = _build_setup(nc, kb, w, wbf)
        kb.barrier()
        if 1 in phases:
            _phase1(nc, kb, S, xin, w, wbf, xd_hbm, zm_hbm)
        if 2 in phases:
            _phase2(nc, kb, S, w, wbf, xd_hbm, zm_hbm, x1_hbm)
            kb.barrier()
        if 3 in phases:
            _phase_ffn(nc, kb, S, w, wbf, x1_hbm, x2_hbm, None, tag="f1")
            kb.barrier()
        if 4 in phases:
            _phase4(nc, kb, S, w, wbf, x2_hbm, x3_hbm)
            kb.barrier()
        if 5 in phases:
            _phase_ffn(nc, kb, S, w, wbf, x3_hbm, None, out, tag="f2")
            kb.barrier()
        for grp in kb.gq_sems:
            kb.wait_gq(nc.gpsimd, grp)
    return nc


def make_in_map(inputs, core):
    b0 = core * NB
    return {
        "xin": np.ascontiguousarray(inputs["L0_attn_a_Z"][b0:b0 + NB]),
        "in_proj": inputs["L0_attn_a_in_proj_w"],
        "conv_w": inputs["L0_attn_a_conv_w"],
        "conv_b": inputs["L0_attn_a_conv_b"],
        "x_proj": inputs["L0_attn_a_x_proj_w"],
        "dt_proj": inputs["L0_attn_a_dt_proj_w"],
        "dt_b": inputs["L0_attn_a_dt_proj_b"],
        "A_log": inputs["L0_attn_a_A_log"],
        "Dv": inputs["L0_attn_a_D"],
        "out_proj": inputs["L0_attn_a_out_proj_w"],
        "mlp_w1": inputs["L0_attn_a_mlp_w1"],
        "mlp_b1": inputs["L0_attn_a_mlp_b1"],
        "mlp_w2": inputs["L0_attn_a_mlp_w2"],
        "mlp_b2": inputs["L0_attn_a_mlp_b2"],
        "norm1_w": inputs["L0_attn_a_norm1_w"],
        "Wb": inputs["L0_attn_b_W"],
        "ln_g": inputs["L0_attn_b_ln_gamma"],
        "ln_b": inputs["L0_attn_b_ln_beta"],
        "eta": np.reshape(inputs["L0_attn_b_eta"], (1, 1)),
        "lam": np.reshape(inputs["L0_attn_b_lam_logit"], (1, 1)),
        "W1": inputs["L0_ffn_W1"],
        "b1": inputs["L0_ffn_b1"],
        "W2": inputs["L0_ffn_W2"],
        "b2": inputs["L0_ffn_b2"],
        "Wg": inputs["L0_ffn_Wg"],
        "bg": np.reshape(inputs["L0_ffn_bg"], (NE, 1)),
    }


def kernel(**inputs):
    inputs = {k: np.asarray(v, dtype=np.float32) for k, v in inputs.items()}
    nc = build_program(debug=False, phases=PHASES)
    in_maps = [make_in_map(inputs, c) for c in range(8)]
    res = run_bass_kernel_spmd(nc, in_maps, core_ids=list(range(8)))
    return np.concatenate([np.asarray(r["out"]) for r in res.results], axis=0)
